# version 09
# baseline (speedup 1.0000x reference)
import contextlib
import numpy as np
import concourse.bass as bass
import concourse.mybir as mybir
from concourse.aot_env import allow_ici_all_to_all
from concourse.bass_utils import run_bass_kernel_spmd

F32 = mybir.dt.float32
BF16 = mybir.dt.bfloat16
I32 = mybir.dt.int32
AF = mybir.ActivationFunctionType
ALU = mybir.AluOpType
AX = mybir.AxisListType.X

B, T, D = 2, 8192, 1024
H, DK, DV = 8, 256, 512
FH = 3072
QS = 1.0 / 127.0
NCORE = 8
C = 128
U = 1024
NU = T // U
LT = B * T // NCORE
NTT = LT // 128
NK = D // 128
NM = FH // 128
LN_EPS = 1e-5
RMS_EPS = 1e-6
GN_EPS = 1e-5
QKN_EPS = 1e-6

import os
DEBUG = 0
TRUNC = int(os.environ.get("KTRUNC", "99"))
P2U = int(os.environ.get("KP2U", "99"))
P2J = int(os.environ.get("KP2J", "99"))
P2CC = int(os.environ.get("KP2CC", "1"))
B1W = int(os.environ.get("KB1W", "1"))


class Sem:
    def __init__(self, handle):
        self.h = handle
        self.n = 0

    def bump(self, k):
        self.n += k
        return self.n


class Emit:

    ENGS = ("sync", "vector", "scalar", "tensor", "gpsimd")
    RING = 6

    def __init__(self):
        self.code = {k: [] for k in self.ENGS}
        self.sems = {}
        self.rings = {}
        self.ridx = {}

    def raw(self, eng, fn):
        self.code[eng].append(fn)

    def wait(self, eng, sem, val):
        s = self.sems[sem]
        self.code[eng].append(lambda e, s=s, val=val: e.wait_ge(s.h, val))

    def wait_tok(self, eng, tok):
        s, n = tok
        self.code[eng].append(lambda e, s=s, n=n: e.wait_ge(s.h, n))

    def op(self, eng, fn, inc=None, k=1):
        if inc is None:
            self.code[eng].append(fn)
            return None
        s = self.sems[inc]
        n = s.bump(k)
        self.code[eng].append(lambda e, fn=fn, s=s, k=k: fn(e).then_inc(s.h, k))
        return n

    def dma(self, eng, out, in_, sem=None, **kw):
        q = sem or ("gdma" if eng == "gpsimd" else "dma")
        ring = self.rings[q]
        i = self.ridx[q] % len(ring)
        self.ridx[q] += 1
        s = ring[i]
        if s.n > 0:
            self.code[eng].append(lambda e, s=s, n=s.n: e.wait_ge(s.h, n))
        n = s.bump(16)
        self.code[eng].append(
            lambda e, out=out, in_=in_, s=s, kw=kw: e.dma_start(out=out, in_=in_, **kw).then_inc(s.h, 16))
        return (s, n)

    def wait_queue(self, eng, q):
        for s in self.rings[q]:
            if s.n > 0:
                self.code[eng].append(lambda e, s=s, n=s.n: e.wait_ge(s.h, n))

    def barrier(self, engines=None):
        snap = [(s, s.n) for s in self.sems.values() if s.n > 0]
        for eng in engines or self.ENGS:
            for s, n in snap:
                self.code[eng].append(lambda e, s=s, n=n: e.wait_ge(s.h, n))

    def run(self, eng_name, eng):
        for fn in self.code[eng_name]:
            fn(eng)


def build_program():
    nc = bass.Bass("TRN2", target_bir_lowering=False, debug=False)

    x_sh = nc.dram_tensor("x_sh", [LT, D], F32, kind="ExternalInput").ap()
    w_gate = nc.dram_tensor("w_gate", [FH, D], I32, kind="ExternalInput").ap()
    w_up = nc.dram_tensor("w_up", [FH, D], I32, kind="ExternalInput").ap()
    w_down = nc.dram_tensor("w_down", [D, FH], I32, kind="ExternalInput").ap()
    norm_w = nc.dram_tensor("norm_w", [1, D], F32, kind="ExternalInput").ap()
    w_head = nc.dram_tensor("w_head", [1792, D], F32, kind="ExternalInput").ap()
    conv_w = nc.dram_tensor("conv_w", [128, 32], F32, kind="ExternalInput").ap()
    o_w = nc.dram_tensor("o_w", [D, H * DV], F32, kind="ExternalInput").ap()
    o_norm = nc.dram_tensor("o_norm", [1, H * DV], F32, kind="ExternalInput").ap()
    adt = nc.dram_tensor("adt", [1, 2], F32, kind="ExternalInput").ap()

    out_sh = nc.dram_tensor("out_sh", [LT, D], F32, kind="ExternalOutput").ap()

    wg_bf = nc.dram_tensor("wg_bf", [FH, D], BF16, kind="Internal").ap()
    wu_bf = nc.dram_tensor("wu_bf", [FH, D], BF16, kind="Internal").ap()
    wd_bf = nc.dram_tensor("wd_bf", [D, FH], BF16, kind="Internal").ap()
    wh_bf = nc.dram_tensor("wh_bf", [1792, D], BF16, kind="Internal").ap()
    ow_bf = nc.dram_tensor("ow_bf", [D, H * DV], BF16, kind="Internal").ap()
    ow_sc = nc.dram_tensor("ow_sc", [D, H * DV], BF16, kind="Internal").ap()
    u_scr = nc.dram_tensor("u_scr", [LT, D], F32, kind="Internal").ap()
    x3_scr = nc.dram_tensor("x3_scr", [LT, D], F32, kind="Internal").ap()
    ln2_scr = nc.dram_tensor("ln2_scr", [LT, 2], F32, kind="Internal").ap()
    hT_sh = nc.dram_tensor("hT_sh", [D, LT], BF16, kind="Internal").ap()
    hT_full = nc.dram_tensor("hT_full", [NCORE * D, LT], BF16, kind="Internal").ap()
    oA_in = [nc.dram_tensor(f"oA_in{u}", [2048, DV], BF16, kind="Internal").ap()
             for u in range(NU)]
    oA_out = [nc.dram_tensor(f"oA_out{u}", [2048, DV], BF16, kind="Internal").ap()
              for u in range(NU)]

    dbg = {}
    if DEBUG == 1:
        dbg["u_dbg"] = nc.dram_tensor("u_dbg", [LT, D], F32, kind="ExternalOutput").ap()
        dbg["h_dbg"] = nc.dram_tensor("h_dbg", [NCORE * D, LT], BF16, kind="ExternalOutput").ap()
        dbg["xn1T_dbg"] = nc.dram_tensor("xn1T_dbg", [128, NK * U], BF16, kind="ExternalOutput").ap()
        dbg["xn1_dbg"] = nc.dram_tensor("xn1_dbg", [LT, D], BF16, kind="ExternalOutput").ap()
        dbg["st_dbg"] = nc.dram_tensor("st_dbg", [NTT * 128, 48], F32, kind="ExternalOutput").ap()
        dbg["hffT_dbg"] = nc.dram_tensor("hffT_dbg", [128, NM * U], BF16, kind="ExternalOutput").ap()
    if DEBUG == 2:
        dbg["o_dbg"] = nc.dram_tensor("o_dbg", [NU * 2048, DV], BF16, kind="ExternalOutput").ap()

    E = Emit()

    with (
        nc.sbuf_tensor("ident_bf", [128, 128], BF16) as ident_bf,
        nc.sbuf_tensor("ident_f32", [128, 128], F32) as ident_f32,
        nc.sbuf_tensor("ones_row", [1, 128], BF16) as ones_row,
        nc.sbuf_tensor("onesr_f32", [1, 128], F32) as onesr_f32,
        nc.sbuf_tensor("onesc_bf", [128, 1], BF16) as onesc_bf,
        nc.sbuf_tensor("onec_f32", [1, 1], F32) as onec_f32,
        nc.sbuf_tensor("mask_st", [128, 128], F32) as mask_st,
        nc.sbuf_tensor("mask_ts", [128, 128], F32) as mask_ts,
        nc.sbuf_tensor("normw_rep", [128, D], F32) as normw_rep,
        nc.sbuf_tensor("convw", [128, 32], F32) as convw,
        nc.sbuf_tensor("adt_sb", [1, 2], F32) as adt_sb,
        nc.sbuf_tensor("negexpA", [1, 1], F32) as negexpA,
        nc.semaphore("s_t") as _s4,
        nc.semaphore("s_v") as _s5,
        nc.semaphore("s_a") as _s6,
        nc.semaphore("s_g") as _s7,
        nc.semaphore("s_cc") as _s8,
        contextlib.ExitStack() as _stk,
        nc.Block() as block,
    ):
        for nm, h in [("t", _s4), ("v", _s5), ("a", _s6), ("g", _s7), ("cc", _s8)]:
            E.sems[nm] = Sem(h)
        for q in ("dma", "gdma", "adma"):
            E.rings[q] = []
            E.ridx[q] = 0
            for r in range(Emit.RING):
                h = _stk.enter_context(nc.semaphore(f"s_{q}{r}"))
                s = Sem(h)
                E.rings[q].append(s)
                E.sems[f"{q}{r}"] = s

        ctx = dict(
            nc=nc, E=E, ident_bf=ident_bf, ident_f32=ident_f32,
            ones_row=ones_row, onesr_f32=onesr_f32, onesc_bf=onesc_bf,
            onec_f32=onec_f32, mask_st=mask_st, mask_ts=mask_ts,
            normw_rep=normw_rep, convw=convw, adt_sb=adt_sb, negexpA=negexpA,
            x_sh=x_sh, w_head=w_head, conv_w=conv_w, o_w=o_w, o_norm=o_norm,
            adt=adt, norm_w=norm_w,
            wg_bf=wg_bf, wu_bf=wu_bf, wd_bf=wd_bf, wh_bf=wh_bf, ow_bf=ow_bf,
            ow_sc=ow_sc, u_scr=u_scr, x3_scr=x3_scr, ln2_scr=ln2_scr, hT_sh=hT_sh,
            hT_full=hT_full, oA_in=oA_in, oA_out=oA_out, out_sh=out_sh,
            w_gate=w_gate, w_up=w_up, w_down=w_down, dbg=dbg,
        )

        emit_prologue(ctx)
        emit_p1(ctx)

        E.barrier(["gpsimd"])
        if TRUNC >= 50:
            n_ag1 = E.op("gpsimd", lambda g: g.collective_compute(
                "AllGather", ALU.bypass, replica_groups=[list(range(NCORE))],
                ins=[hT_sh], outs=[hT_full]), inc="cc")
            ctx["n_ag1"] = n_ag1

        if DEBUG == 1:
            if TRUNC >= 50:
                E.wait("gpsimd", "cc", ctx["n_ag1"])
                E.dma("gpsimd", dbg["h_dbg"], hT_full)
            E.wait_queue("gpsimd", "gdma")
            E.dma("gpsimd", out_sh, u_scr)
            E.dma("gpsimd", dbg["u_dbg"], u_scr) if TRUNC < 40 else None
            E.wait_queue("gpsimd", "gdma")
        else:
            emit_p2(ctx)
            if DEBUG == 2:
                E.wait("gpsimd", "cc", ctx["n_a2a"][NU - 1])
                for u in range(NU):
                    E.dma("gpsimd", dbg["o_dbg"][u * 2048:(u + 1) * 2048, :], oA_out[u])
                E.wait_queue("gpsimd", "gdma")
                E.dma("gpsimd", out_sh, u_scr)
                E.wait_queue("gpsimd", "gdma")
            else:
                emit_p3(ctx)
                E.barrier()

        @block.sync
        def _(sync):
            E.run("sync", sync)

        @block.vector
        def _(vector):
            E.run("vector", vector)

        @block.scalar
        def _(scalar):
            E.run("scalar", scalar)

        @block.tensor
        def _(tensor):
            E.run("tensor", tensor)

        @block.gpsimd
        def _(gpsimd):
            E.run("gpsimd", gpsimd)

    return nc


def emit_prologue(ctx):
    nc, E = ctx["nc"], ctx["E"]

    n1 = E.op("gpsimd", lambda g: g.memset(ctx["ident_bf"][:], 0.0), inc="g")
    n1b = E.op("gpsimd", lambda g: g.memset(ctx["ident_f32"][:], 0.0), inc="g")
    E.wait("gpsimd", "g", n1b)
    E.op("gpsimd", lambda g: g.affine_select(
        out=ctx["ident_bf"][:], in_=ctx["ident_bf"][:],
        compare_op=ALU.not_equal, fill=1.0, base=0,
        pattern=[[-1, 128]], channel_multiplier=1))
    E.op("gpsimd", lambda g: g.affine_select(
        out=ctx["ident_f32"][:], in_=ctx["ident_f32"][:],
        compare_op=ALU.not_equal, fill=1.0, base=0,
        pattern=[[-1, 128]], channel_multiplier=1))
    E.op("gpsimd", lambda g: g.memset(ctx["mask_st"][:], 0.0))
    E.op("gpsimd", lambda g: g.memset(ctx["mask_ts"][:], 0.0))
    ng_mm = E.op("gpsimd", lambda g: g.memset(ctx["ones_row"][:], 1.0), inc="g")
    E.wait("gpsimd", "g", ng_mm)
    E.op("gpsimd", lambda g: g.affine_select(
        out=ctx["mask_st"][:], in_=ctx["mask_st"][:], compare_op=ALU.is_ge,
        fill=-1e30, base=0, pattern=[[1, 128]], channel_multiplier=-1))
    E.op("gpsimd", lambda g: g.affine_select(
        out=ctx["mask_ts"][:], in_=ctx["mask_ts"][:], compare_op=ALU.is_gt,
        fill=-1e30, base=0, pattern=[[-1, 128]], channel_multiplier=1))
    E.op("gpsimd", lambda g: g.memset(ctx["ones_row"][:], 1.0))
    E.op("gpsimd", lambda g: g.memset(ctx["onesr_f32"][:], 1.0))
    E.op("gpsimd", lambda g: g.memset(ctx["onesc_bf"][:], 1.0))
    ctx["n_consts"] = E.op("gpsimd", lambda g: g.memset(ctx["onec_f32"][:], 1.0), inc="g")

    ctx["tok_wg"] = E.dma("gpsimd", ctx["wg_bf"], ctx["w_gate"])
    ctx["tok_wu"] = E.dma("gpsimd", ctx["wu_bf"], ctx["w_up"])
    ctx["tok_wd"] = E.dma("gpsimd", ctx["wd_bf"], ctx["w_down"])
    ctx["tok_wh"] = E.dma("gpsimd", ctx["wh_bf"], ctx["w_head"])
    tok_ow = E.dma("gpsimd", ctx["ow_bf"], ctx["o_w"])
    ctx["tok_ow"] = tok_ow

    ctx["tok_normw"] = E.dma("sync", ctx["normw_rep"][:], ctx["norm_w"].broadcast_to([128, D]))
    tok_adt = E.dma("sync", ctx["adt_sb"][:], ctx["adt"])
    ctx["tok_convw"] = E.dma("sync", ctx["convw"][:, 0:32], ctx["conv_w"])

    E.wait_tok("scalar", tok_adt)
    E.op("scalar", lambda a: a.activation(ctx["negexpA"][:], ctx["adt_sb"][:, 0:1], AF.Exp))
    E.op("scalar", lambda a: a.drain())
    E.op("scalar", lambda a: a.mul(ctx["negexpA"][:], ctx["negexpA"][:], -1.0))
    ctx["n_negexpA"] = E.op("scalar", lambda a: a.drain(), inc="a")

    with nc.sbuf_tensor("pro_owt", [128, H * DV], BF16) as owt, \
         nc.sbuf_tensor("pro_onw", [128, H * DV], BF16) as onw:
        n_onw = E.dma("gpsimd", onw[:], ctx["o_norm"].broadcast_to([128, H * DV]))
        for i in range(8):
            if i == 0:
                E.wait_tok("sync", tok_ow)
            n_ld = E.dma("sync", owt[:], ctx["ow_bf"][i * 128:(i + 1) * 128, :])
            E.wait_tok("vector", n_ld)
            E.wait_tok("vector", n_onw)
            n_mul = E.op("vector", lambda v: v.tensor_mul(owt[:], owt[:], onw[:]), inc="v")
            E.wait("gpsimd", "v", n_mul)
            n_st = E.dma("gpsimd", ctx["ow_sc"][i * 128:(i + 1) * 128, :], owt[:])
            ctx.setdefault("tok_owsc", []).append(n_st)
            E.wait_tok("sync", n_st)
        


def emit_p1(ctx):
    nc, E = ctx["nc"], ctx["E"]
    x_sh, u_scr, ln2_scr, hT_sh = ctx["x_sh"], ctx["u_scr"], ctx["ln2_scr"], ctx["hT_sh"]

    with contextlib.ExitStack() as _st:
        xb = _st.enter_context(nc.sbuf_tensor("p1_x", [128, 2 * D], F32))
        xn1 = _st.enter_context(nc.sbuf_tensor("p1_xn1", [128, D], BF16))
        hstg = _st.enter_context(nc.sbuf_tensor("p1_hstg", [128, D], BF16))
        scr = _st.enter_context(nc.sbuf_tensor("p1_scr", [128, D], F32))
        xn1T = _st.enter_context(nc.sbuf_tensor("p1_xn1T", [128, NK * U], BF16))
        wgT = _st.enter_context(nc.sbuf_tensor("p1_wgT", [128, NK * FH], BF16))
        wuT = _st.enter_context(nc.sbuf_tensor("p1_wuT", [128, NK * FH], BF16))
        hffT = _st.enter_context(nc.sbuf_tensor("p1_hffT", [128, NM * U], BF16))
        wdTk = _st.enter_context(nc.sbuf_tensor("p1_wdTk", [128, 2 * D], BF16))
        stats = _st.enter_context(nc.sbuf_tensor("p1_stats", [128, 48], F32))
        sgate = _st.enter_context(nc.sbuf_tensor("p1_sg", [128, U], BF16))
        utile = _st.enter_context(nc.sbuf_tensor("p1_u", [128, 2 * D], F32))
        xn2 = _st.enter_context(nc.sbuf_tensor("p1_xn2", [128, D], BF16))
        htile = _st.enter_context(nc.sbuf_tensor("p1_h", [128, D], BF16))
        ps_big = _st.enter_context(nc.psum_tensor("p1_ps_big", [128, 2048], F32))
        ps_up2 = _st.enter_context(nc.psum_tensor("p1_ps_up2", [128, 1024], F32))
        ps_t = _st.enter_context(nc.psum_tensor("p1_ps_t", [128, 512], BF16))
        ng0 = E.op("gpsimd", lambda g: g.memset(stats[:], 0.0), inc="g")
        E.wait("vector", "g", ng0)
        E.wait("scalar", "g", ng0)
        if TRUNC < 1:
            return
        E.wait("tensor", "g", ctx["n_consts"])
        E.wait_tok("sync", ctx["tok_wg"])
        E.wait_tok("sync", ctx["tok_wu"])
        E.wait_tok("sync", ctx["tok_wd"])
        wu_toks = []
        for k in range(NK):
            wu_toks.append(E.dma("sync", wgT[:, k * FH:(k + 1) * FH],
                  ctx["wg_bf"][:, k * 128:(k + 1) * 128], transpose=True))
            wu_toks.append(E.dma("sync", wuT[:, k * FH:(k + 1) * FH],
                          ctx["wu_bf"][:, k * 128:(k + 1) * 128], transpose=True))

        if TRUNC < 2:
            return
        tok_xb = [None, None]
        rd_xb = [0, 0]
        rd_xb_a = [0, 0]
        for half in range(2):
            hoff = half * 8
            E.barrier()
            if half == 1:
                wu_toks = []
                for k in range(NK):
                    wu_toks.append(E.dma("sync", wuT[:, k * FH:(k + 1) * FH],
                                   ctx["wu_bf"][:, k * 128:(k + 1) * 128],
                                   transpose=True))

            chain_v = [0, 0]
            stA = {}

            def finishB(i):
                sl = i % 2
                nv0, na0 = stA[i]
                n_rstd = emit_ln_stats(ctx, stats, nv0, na0, D, LN_EPS,
                                       only=sl, war=False)
                chain_v[sl] = E.sems["v"].n
                if TRUNC < 21:
                    return
                E.wait("vector", "a", n_rstd)
                nv = E.op("vector", lambda v, sl=sl: v.tensor_scalar(
                    xn1[:], xb[:, sl * D:(sl + 1) * D],
                    stats[:, 16 + sl:17 + sl], stats[:, 40 + sl:41 + sl],
                    ALU.subtract, ALU.mult), inc="v")
                rd_xb[sl] = nv
                E.wait("sync", "v", nv)
                if DEBUG == 1:
                    E.wait("gpsimd", "v", nv)
                    t_x1 = E.dma("gpsimd", ctx["dbg"]["xn1_dbg"][(hoff + i) * 128:(hoff + i + 1) * 128, :], xn1[:])
                    n_sd = E.dma("gpsimd", ctx["dbg"]["st_dbg"][(hoff + i) * 128:(hoff + i + 1) * 128, :], stats[:])
                    E.wait_tok("vector", t_x1)
                    E.wait_tok("vector", n_sd)
                    E.wait_tok("scalar", n_sd)
                if TRUNC < 22:
                    return
                emit_transpose_row(ctx, src=xn1, dst=xn1T, dst_cols=[
                    (k * U + i * 128) for k in range(NK)], nv=nv, ps_t=ps_t)

            for i in range(8):
                sl = i % 2
                if tok_xb[sl] is not None:
                    E.wait_tok("sync", tok_xb[sl])
                if rd_xb[sl]:
                    E.wait("sync", "v", rd_xb[sl])
                if rd_xb_a[sl]:
                    E.wait("sync", "a", rd_xb_a[sl])
                n_ldx = E.dma("sync", xb[:, sl * D:(sl + 1) * D],
                              x_sh[(hoff + i) * 128:(hoff + i + 1) * 128, :])
                tok_xb[sl] = n_ldx
                E.wait_tok("vector", n_ldx)
                E.op("vector", lambda v, sl=sl: v.reduce_sum(
                    stats[:, sl:sl + 1], xb[:, sl * D:(sl + 1) * D], axis=AX))
                nv0 = E.op("vector", lambda v: v.drain(), inc="v")
                E.wait_tok("scalar", n_ldx)
                E.wait("scalar", "v", chain_v[sl])
                na0 = E.op("scalar", lambda a, sl=sl: a.activation(
                    scr[:], xb[:, sl * D:(sl + 1) * D], AF.Square,
                    accum_out=stats[:, 8 + sl:9 + sl]), inc="a")
                E.op("scalar", lambda a: a.drain())
                rd_xb_a[sl] = na0
                stA[i] = (nv0, na0)
                if i > 0:
                    finishB(i - 1)
            finishB(7)

            if DEBUG == 1 and half == 0:
                E.barrier(["gpsimd"])
                E.dma("gpsimd", ctx["dbg"]["xn1T_dbg"], xn1T[:])
            if TRUNC < 30:
                continue
            for _t in wu_toks:
                E.wait_tok("tensor", _t)
            E.wait("tensor", "a", E.sems["a"].n)
            na_sil = None
            nv_last = None
            nv_up = [None, None]
            for m in range(2 if TRUNC == 30 else NM):
                upps = ps_big[:, 1024:2048] if m % 2 == 0 else ps_up2[:, 0:1024]
                if na_sil is not None:
                    E.wait("tensor", "a", na_sil)
                for hn in range(2):
                    for k in range(NK):
                        E.op("tensor", lambda t, m=m, k=k, hn=hn: t.matmul(
                            ps_big[:, hn * 512:hn * 512 + 512],
                            wgT[:, k * FH + m * 128:k * FH + (m + 1) * 128],
                            xn1T[:, k * U + hn * 512:k * U + hn * 512 + 512],
                            start=(k == 0), stop=(k == NK - 1)))
                if nv_up[m % 2] is not None:
                    E.wait("tensor", "v", nv_up[m % 2])
                for hn in range(2):
                    for k in range(NK):
                        ntl = E.op("tensor", lambda t, m=m, k=k, hn=hn, upps=upps: t.matmul(
                            upps[:, hn * 512:hn * 512 + 512],
                            wuT[:, k * FH + m * 128:k * FH + (m + 1) * 128],
                            xn1T[:, k * U + hn * 512:k * U + hn * 512 + 512],
                            start=(k == 0), stop=(k == NK - 1)), inc="t")
                nt = ntl
                E.wait("scalar", "t", nt)
                if nv_last is not None:
                    E.wait("scalar", "v", nv_last)
                E.op("scalar", lambda a: a.activation(sgate[:], ps_big[:, 0:U],
                                                      AF.Silu, scale=QS))
                na_sil = E.op("scalar", lambda a: a.drain(), inc="a")
                E.wait("vector", "a", na_sil)
                nv_up[m % 2] = E.op("vector", lambda v, m=m, upps=upps: v.scalar_tensor_tensor(
                    hffT[:, m * U:(m + 1) * U], upps[:], QS, sgate[:],
                    ALU.mult, ALU.mult), inc="v")
                nv_last = nv_up[m % 2]

            if DEBUG == 1 and half == 0:
                E.barrier(["gpsimd"])
                E.dma("gpsimd", ctx["dbg"]["hffT_dbg"], hffT[:])
            if TRUNC < 40:
                continue
            wd_toks = []
            E.wait("sync", "t", E.sems["t"].n)
            E.wait("scalar", "t", E.sems["t"].n)
            for k in range(NM):
                wd_toks.append(E.dma("sync", wuT[:, k * 1024:(k + 1) * 1024],
                                     ctx["wd_bf"][:, k * 128:(k + 1) * 128],
                                     transpose=True))
            E.wait("tensor", "a", E.sems["a"].n)
            for p in range(4):
                i0, i1 = 2 * p, 2 * p + 1
                E.wait("tensor", "v", E.sems["v"].n)
                for k in range(NM):
                    if p == 0:
                        E.wait_tok("tensor", wd_toks[k])
                    for j, i in ((0, i0), (1, i1)):
                        for hn in range(2):
                            ntl = E.op("tensor", lambda t, k=k, j=j, i=i, hn=hn: t.matmul(
                                ps_big[:, j * 1024 + hn * 512:j * 1024 + hn * 512 + 512],
                                hffT[:, k * U + i * 128:k * U + (i + 1) * 128],
                                wuT[:, k * 1024 + hn * 512:k * 1024 + hn * 512 + 512],
                                start=(k == 0), stop=(k == NM - 1)), inc="t")
                nt_d = ntl
                for j, i in ((0, i0), (1, i1)):
                    if tok_xb[j] is not None:
                        E.wait_tok("sync", tok_xb[j])
                    if rd_xb[j]:
                        E.wait("sync", "v", rd_xb[j])
                    n_rx = E.dma("sync", xb[:, j * D:(j + 1) * D],
                                 x_sh[(hoff + i) * 128:(hoff + i + 1) * 128, :])
                    tok_xb[j] = n_rx
                    E.wait("vector", "t", nt_d)
                    E.wait_tok("vector", n_rx)
                    nv_u = E.op("vector", lambda v, j=j: v.scalar_tensor_tensor(
                        utile[:, j * D:(j + 1) * D], ps_big[:, j * 1024:(j + 1) * 1024],
                        0.5 * QS, xb[:, j * D:(j + 1) * D],
                        ALU.mult, ALU.add), inc="v")
                    rd_xb[j] = nv_u
                    E.wait("gpsimd", "v", nv_u)
                    E.dma("gpsimd", u_scr[(hoff + i) * 128:(hoff + i + 1) * 128, :],
                          utile[:, j * D:(j + 1) * D])
                    if DEBUG == 1:
                        E.dma("gpsimd",
                              ctx["dbg"]["u_dbg"][(hoff + i) * 128:(hoff + i + 1) * 128, :],
                              utile[:, j * D:(j + 1) * D])
                    E.op("vector", lambda v: v.drain())
                    E.op("vector", lambda v, j=j: v.reduce_sum(
                        stats[:, 0:1], utile[:, j * D:(j + 1) * D], axis=AX))
                    nv1 = E.op("vector", lambda v: v.drain(), inc="v")
                    E.wait("scalar", "v", nv_u)
                    E.wait("scalar", "v", E.sems["v"].n)
                    na1 = E.op("scalar", lambda a, j=j: a.activation(
                        scr[:], utile[:, j * D:(j + 1) * D], AF.Square,
                        accum_out=stats[:, 8:9]), inc="a")
                    E.op("scalar", lambda a: a.drain())
                    n_r2 = emit_ln_stats(ctx, stats, nv1, na1, D, LN_EPS, only=0)
                    E.wait("vector", "a", n_r2)
                    E.op("vector", lambda v, j=j: v.tensor_scalar(
                        xn2[:], utile[:, j * D:(j + 1) * D],
                        stats[:, 16:17], stats[:, 40:41],
                        ALU.subtract, ALU.mult))
                    E.wait_queue("vector", "gdma")
                    E.op("vector", lambda v: v.tensor_copy(stats[:, 44:45], stats[:, 16:17]))
                    E.op("vector", lambda v: v.tensor_copy(stats[:, 45:46], stats[:, 40:41]))
                    nv2 = E.op("vector", lambda v: v.drain(), inc="v")
                    E.wait("gpsimd", "v", nv2)
                    E.dma("gpsimd", ln2_scr[(hoff + i) * 128:(hoff + i + 1) * 128, :],
                          stats[:, 44:46])
                    E.wait("scalar", "v", nv2)
                    E.wait("scalar", "v", E.sems["v"].n)
                    na2 = E.op("scalar", lambda a: a.activation(
                        scr[:], xn2[:], AF.Square, accum_out=stats[:, 46:47]), inc="a")
                    E.op("scalar", lambda a: a.drain())
                    E.wait("vector", "a", na2)
                    E.op("vector", lambda v: v.tensor_scalar(
                        stats[:, 46:47], stats[:, 46:47], 1.0 / D, RMS_EPS,
                        ALU.mult, ALU.add))
                    E.op("vector", lambda v: v.drain())
                    E.op("vector", lambda v: v.reciprocal(stats[:, 47:48], stats[:, 46:47]))
                    nv3 = E.op("vector", lambda v: v.drain(), inc="v")
                    E.wait("scalar", "v", nv3)
                    na3 = E.op("scalar", lambda a: a.activation(
                        stats[:, 47:48], stats[:, 47:48], AF.Sqrt), inc="a")
                    E.op("scalar", lambda a: a.drain())
                    E.wait("vector", "a", na3)
                    E.op("vector", lambda v: v.tensor_scalar(
                        scr[:], xn2[:], stats[:, 47:48], None, ALU.mult))
                    E.op("vector", lambda v: v.drain())
                    nv4 = E.op("vector", lambda v: v.tensor_mul(
                        htile[:], scr[:], ctx["normw_rep"][:]), inc="v")
                    emit_transpose_row(ctx, src=htile, dst=hstg, dst_cols=[
                        k * 128 for k in range(NK)], nv=nv4, ps_t=ps_t)
                    na5 = E.op("scalar", lambda a: a.drain(), inc="a")
                    E.wait("scalar", "a", na5)
                    for k in range(NK):
                        E.dma("scalar",
                              hT_sh[k * 128:(k + 1) * 128,
                                    (hoff + i) * 128:(hoff + i + 1) * 128],
                              hstg[:, k * 128:(k + 1) * 128], sem="adma")
                    E.wait_queue("scalar", "adma")


def emit_transpose_row(ctx, src, dst, dst_cols, nv, ps_t):
    E = ctx["E"]
    E.wait("tensor", "v", nv)
    for k in range(NK):
        slot = 0
        E.wait("tensor", "a", E.sems["a"].n)
        nt = E.op("tensor", lambda t, k=k, slot=slot: t.transpose(
            ps_t[:, slot * 128:slot * 128 + 128],
            src[:, k * 128:(k + 1) * 128], ctx["ident_bf"][:]), inc="t")
        E.wait("scalar", "t", nt)
        na = E.op("scalar", lambda a, k=k, slot=slot: a.copy(
            dst[:, dst_cols[k]:dst_cols[k] + 128],
            ps_t[:, slot * 128:slot * 128 + 128]), inc="a")
    E.op("scalar", lambda a: a.drain())


def emit_ln_stats(ctx, stats, nv, na, dim, eps, only=None, war=True):
    E = ctx["E"]
    if only is None:
        sl_s, sl_q = slice(0, 8), slice(8, 16)
        sl_m, sl_v, sl_t, sl_r = slice(16, 24), slice(24, 32), slice(32, 40), slice(40, 48)
    else:
        i = only
        sl_s, sl_q = slice(i, i + 1), slice(8 + i, 9 + i)
        sl_m, sl_v, sl_t, sl_r = (slice(16 + i, 17 + i), slice(24 + i, 25 + i),
                                  slice(32 + i, 33 + i), slice(40 + i, 41 + i))
    E.wait("vector", "v", nv)
    E.wait("vector", "a", max(na, E.sems["a"].n) if war else na)
    E.op("vector", lambda v: v.tensor_scalar_mul(stats[:, sl_m], stats[:, sl_s], 1.0 / dim))
    E.op("vector", lambda v: v.tensor_scalar_mul(stats[:, sl_v], stats[:, sl_q], 1.0 / dim))
    E.op("vector", lambda v: v.drain())
    E.op("vector", lambda v: v.tensor_mul(stats[:, sl_t], stats[:, sl_m], stats[:, sl_m]))
    E.op("vector", lambda v: v.drain())
    E.op("vector", lambda v: v.tensor_sub(stats[:, sl_v], stats[:, sl_v], stats[:, sl_t]))
    E.op("vector", lambda v: v.drain())
    E.op("vector", lambda v: v.tensor_scalar_add(stats[:, sl_v], stats[:, sl_v], eps))
    E.op("vector", lambda v: v.drain())
    E.op("vector", lambda v: v.reciprocal(stats[:, sl_t], stats[:, sl_v]))
    nvx = E.op("vector", lambda v: v.drain(), inc="v")
    E.wait("scalar", "v", nvx)
    n = E.op("scalar", lambda a: a.activation(stats[:, sl_r], stats[:, sl_t], AF.Sqrt),
             inc="a")
    E.op("scalar", lambda a: a.drain())
    return n



def emit_p2(ctx):
    nc, E = ctx["nc"], ctx["E"]
    hT_full, oA_in, oA_out = ctx["hT_full"], ctx["oA_in"], ctx["oA_out"]

    with contextlib.ExitStack() as _st:
        T = lambda *a: _st.enter_context(nc.sbuf_tensor(*a))
        P = lambda *a: _st.enter_context(nc.psum_tensor(*a))
        UD = U + 3
        whT = T("p2_whT", [128, NK * 1792], BF16)
        hTb = T("p2_hTb", [128, NK * U], BF16)
        rows_all = T("p2_rows", [1, 8 * 1024], F32)
        ps_proj = P("p2_ps_proj", [128, 1024], F32)
        qsq_shared = T("p2_qsq", [128, U], BF16)
        S_ = []
        for s in range(2):
            slv = []
            for pk in range(2):
                dm = T(f"p2_dm{s}{pk}", [128, 512], BF16)
                sv = T(f"p2_sv{s}{pk}", [128, 10 * 128], BF16)
                slv.append(dict(
                    Dst=dm[:, 0:128], Dts=dm[:, 128:256],
                    At=dm[:, 256:384], Ast=dm[:, 384:512],
                    mtmp=T(f"p2_mt{s}{pk}", [128, 256], F32),
                    lam=sv[:, 9 * 128:10 * 128],
                    a=[sv[:, i * 128:(i + 1) * 128] for i in range(5)],
                    x=[sv[:, (5 + i) * 128:(6 + i) * 128] for i in range(2)],
                    m=[sv[:, (7 + i) * 128:(8 + i) * 128] for i in range(2)],
                ))
            qkvp = T(f"p2_qkvp{s}", [128, 8 * UD], BF16)
            d = dict(
                qkvp=qkvp,
                qkvc=T(f"p2_qkvc{s}", [128, 8 * U], BF16),
                qsq=qsq_shared,
                gate_sb=T(f"p2_gate{s}", [128, 4 * U], BF16),
                rows_sb=rows_all[0:1, s * 4 * 1024:(s + 1) * 4 * 1024],
                rows_m=T(f"p2_rowm{s}", [1, 48], F32),
                cerep=T(f"p2_cerep{s}", [128, 24], F32),
                colsBB=T(f"p2_cbb{s}", [128, 16], F32),
                cols_all=T(f"p2_call{s}", [128, 64], F32),
                tails=T(f"p2_tail{s}", [128, 24], BF16),
                pst_all=qkvp[:, 0:1024],
                msb_all=qkvp[:, 1024:2048],
                kbar_all=qkvp[:, 2048:4096],
                vbeta_all=qkvp[:, 4096:8192],
                slv=slv,
                wsb=T(f"p2_w{s}", [128, 512], BF16),
                rsb=T(f"p2_r{s}", [128, 512], BF16),
                osb=T(f"p2_osb{s}", [128, 512], BF16),
                ocp=T(f"p2_ocp{s}", [128, 512], BF16),
                oscr=qsq_shared[:, s * 512:(s + 1) * 512],
                sf0=T(f"p2_sf0{s}", [128, 512], F32),
                sf1=T(f"p2_sf1{s}", [128, 512], F32),
                sb0=T(f"p2_sb0{s}", [128, 512], BF16),
                sb1=T(f"p2_sb1{s}", [128, 512], BF16),
                psB0=ps_proj[:, s * 512:(s + 1) * 512],
                psB1=P(f"p2_psB1_{s}", [128, 512], F32),
                psO=P(f"p2_psO_{s}", [128, 512], F32),
                psX=P(f"p2_psX_{s}", [128, 1024], BF16),
                g={},
            )
            S_.append(d)
        ctx["_p2g"] = {}
        shared = dict(whT=whT, hTb=hTb, ps_proj=ps_proj)

        E.barrier()
        E.wait_tok("sync", ctx["tok_wh"])
        tok_whT = [E.dma("sync", whT[:, k * 1792:(k + 1) * 1792],
                         ctx["wh_bf"][:, k * 128:(k + 1) * 128], transpose=True)
                   for k in range(NK)]
        for t in tok_whT:
            E.wait_tok("tensor", t)

        nz = None
        for s in range(2):
            d = S_[s]
            for buf in (d["sf0"], d["sf1"], d["sb0"], d["sb1"]):
                nz = E.op("gpsimd", lambda g, buf=buf: g.memset(buf[:], 0.0), inc="g")
            nz = E.op("gpsimd", lambda g, d=d: g.memset(
                d["tails"][:], 0.0), inc="g")
        E.wait("vector", "g", nz)
        E.wait("tensor", "g", nz)
        E.wait("scalar", "g", nz)
        E.wait("gpsimd", "cc", ctx["n_ag1"])

        for u in range(NU):
            if u >= P2U:
                ctx.setdefault("n_a2a", {})[u] = E.sems["cc"].n
                continue
            emit_p2_phaseA(ctx, 0, u, S_[0], shared)
            emit_p2_phaseA(ctx, 1, u, S_[1], shared)
            NJ = min(8, P2J)
            queues = {s: [emit_p2_b1(ctx, s, u, j, S_[s], shared)
                          for j in range(NJ)] for s in range(2)}
            active = []
            for s in range(2):
                for _ in range(min(B1W, len(queues[s]))):
                    active.append((s, queues[s].pop(0)))
            while active:
                nxt = []
                for s, gen in active:
                    if next(gen, True) is True:
                        if queues[s]:
                            nxt.append((s, queues[s].pop(0)))
                    else:
                        nxt.append((s, gen))
                active = nxt
            g0 = emit_p2_b2(ctx, 0, u, S_[0], shared)
            g1 = emit_p2_b2(ctx, 1, u, S_[1], shared)
            done0 = done1 = False
            while not (done0 and done1):
                if not done0:
                    done0 = next(g0, True) is True
                if not done1:
                    done1 = next(g1, True) is True
            if P2CC:
                E.wait_queue("gpsimd", "gdma")
                def g_a2a(g, u=u):
                    with allow_ici_all_to_all():
                        return g.collective_compute(
                            "AllToAll", ALU.bypass,
                            replica_groups=[list(range(NCORE))],
                            ins=[oA_in[u]], outs=[oA_out[u]])
                ctx.setdefault("n_a2a", {})[u] = E.op("gpsimd", g_a2a, inc="cc")
            else:
                ctx.setdefault("n_a2a", {})[u] = E.sems["cc"].n


def emit_p2_phaseA(ctx, s, u, L, SH):
    nc, E = ctx["nc"], ctx["E"]
    whT, hTb, ps_proj = SH["whT"], SH["hTb"], SH["ps_proj"]
    qkvp, qkvc, qsq = L["qkvp"], L["qkvc"], L["qsq"]
    gate_sb, rows_m, cerep = L["gate_sb"], L["rows_m"], L["cerep"]
    psB0 = L["psB0"]
    hT_full = ctx["hT_full"]
    G = ctx["_p2g"]
    g = L["g"]
    UD = U + 3
    b = s

    if G.get("hTb_t") is not None:
        E.wait("sync", "t", G["hTb_t"])
        E.wait("scalar", "t", G["hTb_t"])
    tok_h = []
    for k in range(NK):
        toks = []
        for jj in range(NCORE):
            eng, q = (("sync", None) if jj % 2 == 0 else ("scalar", "adma"))
            toks.append(E.dma(
                eng,
                hTb[:, k * U + jj * 128:k * U + (jj + 1) * 128],
                hT_full[jj * 1024 + k * 128: jj * 1024 + (k + 1) * 128,
                        b * 1024 + u * 128: b * 1024 + (u + 1) * 128],
                sem=q))
        tok_h.append(toks)

    if g.get("pm_t") is not None:
        E.wait("vector", "t", g["pm_t"])
    E.wait("vector", "a", E.sems["a"].n)
    nh = None
    for p in range(8):
        nh = E.op("vector", lambda v, p=p: v.tensor_copy(
            qkvp[:, p * UD: p * UD + 3],
            L["tails"][:, p * 3:p * 3 + 3]), inc="v")
    E.wait("scalar", "v", nh)
    g["hist_v"] = nh

    if g.get("qkvp_v") is not None:
        E.wait("scalar", "v", g["qkvp_v"])
    pslot = G.setdefault("pslot", {})
    if g.get("pm_t") is not None:
        E.wait("scalar", "t", g["pm_t"])
    for mi, m in enumerate(list(range(8)) + [12, 13]):
        for hn in range(2):
            sl = (mi * 2 + hn) % 2
            if pslot.get(sl) is not None:
                E.wait("tensor", *pslot[sl])
            for k in range(NK):
                if mi == 0 and hn == 0:
                    for tk in tok_h[k]:
                        E.wait_tok("tensor", tk)
                nt = E.op("tensor", lambda t, m=m, k=k, hn=hn, sl=sl: t.matmul(
                    ps_proj[:, sl * 512:sl * 512 + 512],
                    whT[:, k * 1792 + m * 128:k * 1792 + (m + 1) * 128],
                    hTb[:, k * U + hn * 512:k * U + hn * 512 + 512],
                    start=(k == 0), stop=(k == NK - 1)),
                    inc=("t" if k == NK - 1 else None))
            if m < 8:
                E.wait("scalar", "t", nt)
                na = E.op("scalar", lambda a, m=m, hn=hn, sl=sl: a.copy(
                    qkvp[:, m * UD + 3 + hn * 512: m * UD + 3 + hn * 512 + 512],
                    ps_proj[:, sl * 512:sl * 512 + 512]), inc="a")
                pslot[sl] = ("a", na)
            else:
                blk = 0 if m == 12 else 3 * 1024
                E.wait("vector", "t", nt)
                if g.get("rows_t") is not None:
                    E.wait("vector", "t", g["rows_t"])
                nv = E.op("vector", lambda v, L=L, blk=blk, hn=hn, sl=sl: v.tensor_copy(
                    L["rows_sb"][:, blk + hn * 512:blk + hn * 512 + 512],
                    ps_proj[0:1, sl * 512:sl * 512 + 512]), inc="v")
                pslot[sl] = ("v", nv)

    if g.get("gate_v") is not None:
        E.wait("scalar", "v", g["gate_v"])
    for j in range(8):
        sl = j % 2
        if pslot.get(sl) is not None:
            E.wait("tensor", *pslot[sl])
        for k in range(NK):
            nt = E.op("tensor", lambda t, j=j, k=k, sl=sl: t.matmul(
                ps_proj[:, sl * 512:sl * 512 + 512],
                hTb[:, k * U + j * 128:k * U + (j + 1) * 128],
                whT[:, k * 1792 + 1024:k * 1792 + 1536],
                start=(k == 0), stop=(k == NK - 1)),
                inc=("t" if k == NK - 1 else None))
        E.wait("scalar", "t", nt)
        na = E.op("scalar", lambda a, j=j, sl=sl: a.activation(
            gate_sb[:, j * 512:(j + 1) * 512], ps_proj[:, sl * 512:sl * 512 + 512],
            AF.Silu), inc="a")
        pslot[sl] = ("a", na)
    G["hTb_t"] = E.sems["t"].n

    na_all = E.sems["a"].n
    if g.get("qkvc_t") is not None:
        E.wait("vector", "t", g["qkvc_t"])
    E.wait("vector", "a", na_all)
    for tap in range(4):
        for p in range(8):
            base, ob = p * UD, p * U
            if tap == 0:
                E.op("vector", lambda v, p=p, base=base, ob=ob: v.tensor_scalar(
                    qkvc[:, ob:ob + U], qkvp[:, base:base + U],
                    ctx["convw"][:, p * 4:p * 4 + 1], None, ALU.mult))
            else:
                E.op("vector", lambda v, p=p, base=base, ob=ob, tap=tap:
                     v.scalar_tensor_tensor(
                         qkvc[:, ob:ob + U], qkvp[:, base + tap:base + tap + U],
                         ctx["convw"][:, p * 4 + tap:p * 4 + tap + 1],
                         qkvc[:, ob:ob + U], ALU.mult, ALU.add))
        E.op("vector", lambda v: v.drain())
    nconv = E.op("vector", lambda v: v.drain(), inc="v")
    for p in range(8):
        nconv = E.op("vector", lambda v, p=p: v.tensor_copy(
            L["tails"][:, p * 3:p * 3 + 3],
            qkvp[:, p * UD + U: p * UD + U + 3]), inc="v")
    g["qkvp_v"] = nconv
    for p in range(8):
        E.wait("scalar", "v", nconv)
        E.op("scalar", lambda a, p=p: a.activation(
            qkvc[:, p * U:(p + 1) * U], qkvc[:, p * U:(p + 1) * U], AF.Silu))
        E.op("scalar", lambda a: a.drain())
    na_silu = E.op("scalar", lambda a: a.drain(), inc="a")
    g["qkvc_a"] = na_silu

    BL = 1024
    r0 = L["rows_sb"]
    if g.get("rows_t") is not None:
        E.wait("scalar", "t", g["rows_t"])
        E.wait("vector", "t", g["rows_t"])
    E.wait("scalar", "v", E.sems["v"].n)
    E.op("scalar", lambda a: a.activation(r0[:, BL:2 * BL], r0[:, 0:BL],
                                          AF.Identity, bias=ctx["adt_sb"][0:1, 1:2]))
    E.op("scalar", lambda a: a.drain())
    E.op("scalar", lambda a: a.activation(r0[:, 2 * BL:3 * BL], r0[:, BL:2 * BL], AF.Abs))
    E.op("scalar", lambda a: a.drain())
    E.op("scalar", lambda a: a.activation(r0[:, 2 * BL:3 * BL], r0[:, 2 * BL:3 * BL],
                                          AF.Exp, scale=-1.0))
    E.op("scalar", lambda a: a.drain())
    na_r = E.op("scalar", lambda a: a.activation(
        r0[:, 2 * BL:3 * BL], r0[:, 2 * BL:3 * BL], AF.Ln,
        bias=ctx["onec_f32"][0:1, 0:1]), inc="a")
    E.op("scalar", lambda a: a.drain())
    na_b = E.op("scalar", lambda a: a.activation(
        r0[:, 3 * BL:4 * BL], r0[:, 3 * BL:4 * BL], AF.Tanh, scale=0.5), inc="a")
    E.op("scalar", lambda a: a.drain())
    E.wait("vector", "a", na_b)
    E.op("vector", lambda v: v.tensor_scalar(r0[:, 3 * BL:4 * BL], r0[:, 3 * BL:4 * BL],
                                             0.5, 0.5, ALU.mult, ALU.add))
    E.op("vector", lambda v: v.drain())
    E.op("vector", lambda v: v.tensor_scalar_max(r0[:, BL:2 * BL],
                                                 r0[:, BL:2 * BL], 0.0))
    E.op("vector", lambda v: v.drain())
    E.op("vector", lambda v: v.tensor_add(r0[:, BL:2 * BL],
                                          r0[:, BL:2 * BL], r0[:, 2 * BL:3 * BL]))
    E.op("vector", lambda v: v.drain())
    E.op("vector", lambda v: v.tensor_scalar(
        r0[:, 0:BL], r0[:, BL:2 * BL], ctx["negexpA"][0:1, 0:1], None, ALU.mult))
    E.op("vector", lambda v: v.drain())
    E.op("vector", lambda v: v.tensor_tensor_scan(
        r0[:, 2 * BL:3 * BL], r0[:, 0:BL], r0[:, 0:BL], 0.0, ALU.add, ALU.bypass))
    E.op("vector", lambda v: v.drain())
    E.op("vector", lambda v: v.tensor_scalar_mul(r0[:, BL:2 * BL],
                                                 r0[:, 2 * BL:3 * BL], -1.0))
    g["rows_v"] = E.op("vector", lambda v: v.drain(), inc="v")
    E.op("vector", lambda v: v.memset(rows_m[0:1, 0:1], 0.0))
    E.op("vector", lambda v: v.tensor_copy(
        rows_m[0:1, 1:9], r0[:, 2 * BL + 127:3 * BL:128]))
    E.op("vector", lambda v: v.drain())
    E.op("vector", lambda v: v.tensor_sub(rows_m[0:1, 16:24], rows_m[0:1, 1:9],
                                          rows_m[0:1, 0:8]))
    E.op("vector", lambda v: v.drain())
    E.op("vector", lambda v: v.tensor_scalar_mul(
        rows_m[0:1, 32:40], rows_m[0:1, 0:8], -1.0))
    E.op("vector", lambda v: v.drain())
    nv_nbge = E.op("vector", lambda v: v.tensor_copy(
        rows_m[0:1, 40:48], rows_m[0:1, 1:9]), inc="v")
    E.op("vector", lambda v: v.drain())
    E.wait("scalar", "v", nv_nbge)
    na_ce = E.op("scalar", lambda a: a.activation(rows_m[0:1, 24:32],
                                                  rows_m[0:1, 16:24], AF.Exp), inc="a")
    E.op("scalar", lambda a: a.drain())

    pslot = G["pslot"]
    for which, pts in (("q", (0, 1)), ("k", (2, 3))):
        for sl in (0, 1):
            if pslot.get(sl) is not None:
                E.wait("tensor", *pslot[sl])
        nt_last = None
        for i, p in enumerate(pts):
            E.wait("scalar", "a", E.sems["a"].n)
            if i == 1:
                E.wait("scalar", "t", nt_last)
            na_sq = E.op("scalar", lambda a, p=p: a.activation(
                qsq[:], qkvc[:, p * U:(p + 1) * U], AF.Square), inc="a")
            E.op("scalar", lambda a: a.drain())
            E.wait("tensor", "a", na_sq)
            for hn in range(2):
                nt_last = E.op("tensor", lambda t, i=i, hn=hn: t.matmul(
                    ps_proj[0:1, hn * 512:hn * 512 + 512], ctx["onesc_bf"][:],
                    qsq[:, hn * 512:hn * 512 + 512],
                    start=(i == 0), stop=(i == 1)), inc="t")
        rrow = r0[:, 0:BL]
        E.wait("vector", "t", nt_last)
        E.op("vector", lambda v: v.tensor_scalar(
            rrow, ps_proj[0:1, 0:1024], QKN_EPS, None, ALU.add))
        nv_r2 = E.op("vector", lambda v: v.drain(), inc="v")
        E.wait("scalar", "v", nv_r2)
        E.op("scalar", lambda a: a.activation(rrow, rrow, AF.Ln))
        E.op("scalar", lambda a: a.drain())
        na_e = E.op("scalar", lambda a: a.activation(rrow, rrow, AF.Exp, scale=-0.5),
                    inc="a")
        E.op("scalar", lambda a: a.drain())
        E.wait("vector", "a", na_e)
        if which == "q":
            nv_q = E.op("vector", lambda v: v.tensor_scalar_mul(rrow, rrow, 0.0625),
                        inc="v")
            E.op("vector", lambda v: v.drain())
        else:
            nv_q = E.op("vector", lambda v: v.drain(), inc="v")
        E.wait("tensor", "v", nv_q)
        nt_b = None
        for hn in range(2):
            nt_b = E.op("tensor", lambda t, hn=hn: t.matmul(
                ps_proj[:, hn * 512:hn * 512 + 512], ctx["onesr_f32"][:],
                r0[:, hn * 512:hn * 512 + 512],
                start=True, stop=True), inc="t")
        E.wait("vector", "t", nt_b)
        nv_ap = None
        for p in pts:
            nv_ap = E.op("vector", lambda v, p=p: v.tensor_mul(
                qkvc[:, p * U:(p + 1) * U], qkvc[:, p * U:(p + 1) * U],
                ps_proj[:, 0:1024]), inc="v")
            E.op("vector", lambda v: v.drain())
        pslot[0] = ("v", nv_ap)
        pslot[1] = ("v", nv_ap)
    for sl in (0, 1):
        if pslot.get(sl) is not None:
            E.wait("tensor", *pslot[sl])
    E.wait("tensor", "a", na_ce)
    for j in range(8):
        E.op("tensor", lambda t, j=j: t.transpose(
            psB0[:, 256 + 2 * j:257 + 2 * j],
            r0[:, 2 * BL + j * 128:2 * BL + (j + 1) * 128],
            ctx["ident_f32"][0:1, 0:1]))
        E.op("tensor", lambda t, j=j: t.transpose(
            psB0[:, 257 + 2 * j:258 + 2 * j],
            r0[:, 3 * BL + j * 128:3 * BL + (j + 1) * 128],
            ctx["ident_f32"][0:1, 0:1]))
    nt_bb = E.op("tensor", lambda t: t.matmul(
        psB0[:, 272:296], ctx["onesr_f32"][:], rows_m[0:1, 24:48],
        start=True, stop=True), inc="t")
    E.wait("vector", "t", nt_bb)
    if g.get("cer_v") is not None:
        E.wait("vector", "v", g["cer_v"])
    if g.get("cer_a") is not None:
        E.wait("vector", "a", g["cer_a"])
    E.op("vector", lambda v: v.tensor_copy(L["colsBB"][:], psB0[:, 256:272]))
    nv_bb = E.op("vector", lambda v: v.tensor_copy(cerep[:], psB0[:, 272:296]),
                 inc="v")
    E.op("vector", lambda v: v.drain())
    ca, bb = L["cols_all"], L["colsBB"]
    for j in range(8):
        E.op("vector", lambda v, j=j: v.tensor_scalar_mul(
            ca[:, 8 * j + 3:8 * j + 4], bb[:, 2 * j:2 * j + 1], -1.0))
    nv_nb = E.op("vector", lambda v: v.drain(), inc="v")
    E.wait("scalar", "v", max(nv_bb, nv_nb))
    for j in range(8):
        E.op("scalar", lambda a, j=j: a.activation(
            ca[:, 8 * j:8 * j + 1], bb[:, 2 * j:2 * j + 1], AF.Exp,
            bias=cerep[:, 8 + j:9 + j]))
        E.op("scalar", lambda a, j=j: a.activation(
            ca[:, 8 * j + 2:8 * j + 3], ca[:, 8 * j + 3:8 * j + 4], AF.Exp,
            bias=cerep[:, 16 + j:17 + j]))
    na_cols = E.op("scalar", lambda a: a.drain(), inc="a")
    E.wait("vector", "a", na_cols)
    for j in range(8):
        E.op("vector", lambda v, j=j: v.scalar_tensor_tensor(
            ca[:, 8 * j + 1:8 * j + 2], ca[:, 8 * j:8 * j + 1], -1.0,
            bb[:, 2 * j + 1:2 * j + 2], ALU.mult, ALU.mult))
    nv_cols = E.op("vector", lambda v: v.drain(), inc="v")
    g["cols_v"] = nv_cols
    g["cols_a"] = na_cols
    pslot[s] = ("v", nv_bb)
def emit_p2_b1(ctx, s, u, j, L, SH):
    nc, E = ctx["nc"], ctx["E"]
    g = L["g"]
    bk = j % 2
    bank = L["psB1"] if bk == 0 else L["psO"]
    ka, kv = ("bp_a", "bp_v") if bk == 0 else ("bq_a", "bq_v")
    sl = L["slv"][bk]
    Dst, Dts, At, Ast = sl["Dst"], sl["Dts"], sl["At"], sl["Ast"]
    mtmp, LamRep = sl["mtmp"], sl["lam"]
    a_slots, x_slots, m_slots = sl["a"], sl["x"], sl["m"]
    psX = L["psX"]
    qkvc = L["qkvc"]
    r0 = L["rows_sb"]
    BL = 1024
    Kf = [qkvc[:, 2 * U + j * 128:2 * U + (j + 1) * 128],
          qkvc[:, 3 * U + j * 128:3 * U + (j + 1) * 128]]
    Qf = [qkvc[:, 0 * U + j * 128:0 * U + (j + 1) * 128],
          qkvc[:, 1 * U + j * 128:1 * U + (j + 1) * 128]]
    Vf = [qkvc[:, (4 + i) * U + j * 128:(4 + i) * U + (j + 1) * 128]
          for i in range(4)]
    cerep, bb, ca = L["cerep"], L["colsBB"], L["cols_all"]
    bgcol = bb[:, 2 * j:2 * j + 1]
    betacol = bb[:, 2 * j + 1:2 * j + 2]
    nbgcol = ca[:, 8 * j + 3:8 * j + 4]
    cekcol = ca[:, 8 * j + 2:8 * j + 3]
    PstO = L["pst_all"][:, j * 128:(j + 1) * 128]
    MsbO = L["msb_all"][:, j * 128:(j + 1) * 128]
    kbarO = L["kbar_all"][:, j * 256:(j + 1) * 256]
    vbetaO = L["vbeta_all"][:, j * 512:(j + 1) * 512]

    if g.get(ka) is not None:
        E.wait("tensor", "a", g[ka])
    if g.get(kv) is not None:
        E.wait("tensor", "v", g[kv])
    E.wait("tensor", "v", max(g["rows_v"], g["cols_v"]))
    E.wait("tensor", "a", g["qkvc_a"])
    E.op("tensor", lambda t: t.matmul(
        bank[:, 0:128], ctx["onesr_f32"][:],
        r0[:, 2 * BL + j * 128:2 * BL + (j + 1) * 128], start=True, stop=True))
    E.op("tensor", lambda t: t.matmul(
        bank[:, 128:256], ctx["onesr_f32"][:],
        r0[:, BL + j * 128:BL + (j + 1) * 128], start=True, stop=True))
    E.op("tensor", lambda t: t.matmul(bank[:, 256:384], Kf[0], Kf[0],
                                      start=True, stop=False))
    E.op("tensor", lambda t: t.matmul(bank[:, 256:384], Kf[1], Kf[1],
                                      start=False, stop=True))
    E.op("tensor", lambda t: t.matmul(bank[:, 384:512], Kf[0], Qf[0],
                                      start=True, stop=False))
    nt_kq = E.op("tensor", lambda t: t.matmul(bank[:, 384:512], Kf[1], Qf[1],
                                              start=False, stop=True), inc="t")
    g["rows_t"] = nt_kq
    yield

    E.wait("vector", "t", nt_kq)
    if g.get(f"mt{bk}_a") is not None:
        E.wait("vector", "a", g[f"mt{bk}_a"])
    E.op("vector", lambda v: v.tensor_add(mtmp[:, 0:128], bank[:, 0:128],
                                          ctx["mask_st"][:]))
    nv_mt = E.op("vector", lambda v: v.tensor_add(
        mtmp[:, 128:256], bank[:, 128:256], ctx["mask_ts"][:]), inc="v")
    E.op("vector", lambda v: v.drain())
    yield

    E.wait("scalar", "v", max(nv_mt, g["cols_v"]))
    if g.get(f"dm{bk}_v") is not None:
        E.wait("scalar", "v", g[f"dm{bk}_v"])
    if g.get(f"lam{bk}_v") is not None:
        E.wait("scalar", "v", g[f"lam{bk}_v"])
    E.op("scalar", lambda a: a.activation(Dst, mtmp[:, 0:128], AF.Exp,
                                          bias=nbgcol))
    E.op("scalar", lambda a: a.activation(Dts, mtmp[:, 128:256], AF.Exp,
                                          bias=bgcol))
    na_lam = E.op("scalar", lambda a: a.activation(
        LamRep, bank[:, 0:128], AF.Exp, bias=cerep[:, 8 + j:9 + j]), inc="a")
    E.op("scalar", lambda a: a.drain())
    g[f"mt{bk}_a"] = na_lam
    g["cer_a"] = max(g.get("cer_a") or 0, na_lam)
    yield

    E.wait("vector", "a", na_lam)
    if g.get("pm_t") is not None:
        E.wait("vector", "t", g["pm_t"])
    if g.get(f"sl{bk}_t") is not None:
        E.wait("vector", "t", g[f"sl{bk}_t"])
    E.op("vector", lambda v: v.scalar_tensor_tensor(
        At, bank[:, 256:384], betacol, Dts, ALU.mult, ALU.mult))
    E.op("vector", lambda v: v.drain())
    nv_at = E.op("vector", lambda v: v.tensor_mul(PstO, bank[:, 384:512], Dst),
                 inc="v")
    E.op("vector", lambda v: v.drain())
    E.op("vector", lambda v: v.tensor_mul(Qf[0], Qf[0], LamRep))
    nv_qh = E.op("vector", lambda v: v.tensor_mul(Qf[1], Qf[1], LamRep), inc="v")
    E.op("vector", lambda v: v.drain())
    g[f"lam{bk}_v"] = nv_qh
    g[f"dm{bk}_v"] = nv_at
    g["cer_v"] = max(g.get("cer_v") or 0, nv_at)
    yield

    E.wait("tensor", "v", nv_at)
    if g.get("psx_a") is not None:
        E.wait("tensor", "a", g["psx_a"])
    if g.get("psx_v") is not None:
        E.wait("tensor", "v", g["psx_v"])
    E.op("tensor", lambda t: t.transpose(psX[:, 0:128], At, ctx["ident_bf"][:]))
    E.op("tensor", lambda t: t.transpose(psX[:, 128:256], Kf[0],
                                         ctx["ident_bf"][:]))
    E.op("tensor", lambda t: t.transpose(psX[:, 256:384], Kf[1],
                                         ctx["ident_bf"][:]))
    for i in range(3):
        E.op("tensor", lambda t, i=i: t.transpose(
            psX[:, 384 + i * 128:512 + i * 128], Vf[i], ctx["ident_bf"][:]))
    nt_tr = E.op("tensor", lambda t: t.transpose(
        psX[:, 768:896], Vf[3], ctx["ident_bf"][:]), inc="t")
    E.wait("scalar", "t", nt_tr)
    if g.get(f"sl{bk}_t") is not None:
        E.wait("scalar", "t", g[f"sl{bk}_t"])
    na_ast = E.op("scalar", lambda a: a.copy(Ast, psX[:, 0:128]), inc="a")
    E.op("scalar", lambda a: a.drain())
    E.wait("vector", "t", nt_tr)
    E.op("vector", lambda v: v.tensor_copy(kbarO, psX[:, 128:384]))
    E.op("vector", lambda v: v.tensor_copy(vbetaO, psX[:, 384:896]))
    E.op("vector", lambda v: v.drain())
    E.op("vector", lambda v: v.tensor_scalar(
        kbarO, kbarO, cekcol, None, ALU.mult))
    nv_tr = E.op("vector", lambda v: v.tensor_scalar(
        vbetaO, vbetaO, betacol, None, ALU.mult), inc="v")
    E.op("vector", lambda v: v.drain())
    g["psx_a"] = na_ast
    g["psx_v"] = nv_tr
    g["cer_v"] = max(g.get("cer_v") or 0, nv_tr)
    yield

    a_pows = [At]
    cur_x, cur_a = Ast, At
    last_na = None
    E.wait("tensor", "a", na_ast)
    E.wait("tensor", "v", nv_at)
    if g.get(f"m0{bk}_v") is not None:
        E.wait("scalar", "v", g[f"m0{bk}_v"])
    for lvl in range(6):
        if last_na is not None:
            E.wait("tensor", "a", last_na)
        last = lvl == 5
        new_x = x_slots[lvl % 2]
        if not last:
            E.op("tensor", lambda t, cur_x=cur_x, cur_a=cur_a: t.matmul(
                bank[:, 0:128], cur_a, cur_x, start=True, stop=True))
            nt_sq = E.op("tensor", lambda t, cur_x=cur_x, cur_a=cur_a: t.matmul(
                bank[:, 128:256], cur_x, cur_a, start=True, stop=True), inc="t")
            new_a = a_slots[lvl]
            E.wait("scalar", "t", nt_sq)
            E.op("scalar", lambda a, new_x=new_x: a.copy(new_x, bank[:, 0:128]))
            na_sq = E.op("scalar", lambda a, new_a=new_a: a.copy(
                new_a, bank[:, 128:256]), inc="a")
            E.op("scalar", lambda a: a.drain())
            last_na = na_sq
            a_pows.append(new_a)
            cur_x, cur_a = new_x, new_a
        else:
            nt_sq = E.op("tensor", lambda t, cur_x=cur_x, cur_a=cur_a: t.matmul(
                bank[:, 0:128], cur_a, cur_x, start=True, stop=True), inc="t")
            E.wait("scalar", "t", nt_sq)
            na_sq = E.op("scalar", lambda a, new_x=new_x: a.copy(
                new_x, bank[:, 0:128]), inc="a")
            E.op("scalar", lambda a: a.drain())
            last_na = na_sq
            cur_x = new_x
        yield

    E.wait("vector", "a", last_na)
    if g.get(f"sl{bk}_t") is not None:
        E.wait("vector", "t", g[f"sl{bk}_t"])
    nv_m = E.op("vector", lambda v, cur_x=cur_x: v.tensor_add(
        m_slots[0], cur_x, ctx["ident_bf"][:]), inc="v")
    E.op("vector", lambda v: v.drain())
    g[f"m0{bk}_v"] = nv_m
    m_cur = m_slots[0]
    prods = [a_pows[5], a_pows[4], a_pows[3], a_pows[2], a_pows[1], a_pows[0]]
    E.wait("tensor", "a", last_na)
    nt_p = None
    for pi, Ap in enumerate(prods):
        sign = -1.0 if pi == len(prods) - 1 else 1.0
        E.wait("tensor", "v", nv_m)
        nt_p = E.op("tensor", lambda t, Ap=Ap, m_cur=m_cur: t.matmul(
            bank[:, 256:384], Ap, m_cur, start=True, stop=True), inc="t")
        m_next = MsbO if pi == len(prods) - 1 else m_slots[1 - pi % 2]
        E.wait("vector", "t", nt_p)
        nv_m = E.op("vector", lambda v, m_next=m_next, m_cur=m_cur, sign=sign:
                    v.scalar_tensor_tensor(
                        m_next, bank[:, 256:384], sign, m_cur,
                        ALU.mult, ALU.add), inc="v")
        E.op("vector", lambda v: v.drain())
        m_cur = m_next
        yield
    g[f"sl{bk}_t"] = nt_p
    g[ka] = last_na
    g[kv] = nv_m
    return


def emit_p2_b2(ctx, s, u, L, SH):
    nc, E = ctx["nc"], ctx["E"]
    g = L["g"]
    bankP, bankQ = L["psB1"], L["psO"]
    qkvc, gate_sb = L["qkvc"], L["gate_sb"]
    cerep, ca = L["cerep"], L["cols_all"]
    wsb, rsb, osb, ocp, oscr = L["wsb"], L["rsb"], L["osb"], L["ocp"], L["oscr"]
    sf0, sf1, sb0, sb1 = L["sf0"], L["sf1"], L["sb0"], L["sb1"]
    oA_in = ctx["oA_in"]
    b = s
    NJ = min(8, P2J)
    pend = []
    for j in range(NJ):
        Kf = [qkvc[:, 2 * U + j * 128:2 * U + (j + 1) * 128],
              qkvc[:, 3 * U + j * 128:3 * U + (j + 1) * 128]]
        Qf = [qkvc[:, 0 * U + j * 128:0 * U + (j + 1) * 128],
              qkvc[:, 1 * U + j * 128:1 * U + (j + 1) * 128]]
        PstO = L["pst_all"][:, j * 128:(j + 1) * 128]
        MsbO = L["msb_all"][:, j * 128:(j + 1) * 128]
        kbarO = L["kbar_all"][:, j * 256:(j + 1) * 256]
        vbetaO = L["vbeta_all"][:, j * 512:(j + 1) * 512]
        nblcol = ca[:, 8 * j + 1:8 * j + 2]
        mscol = ca[:, 8 * j + 4:8 * j + 5]
        rscol = ca[:, 8 * j + 5:8 * j + 6]

        if g.get("state_a") is not None:
            E.wait("tensor", "a", g["state_a"])
        if g.get("bp_a") is not None:
            E.wait("tensor", "a", g["bp_a"])
        if g.get("bp_v") is not None:
            E.wait("tensor", "v", g["bp_v"])
        E.op("tensor", lambda t, Kf=Kf: t.matmul(bankP[:], Kf[0], sb0[:],
                                                 start=True, stop=False))
        nt_ks = E.op("tensor", lambda t, Kf=Kf: t.matmul(
            bankP[:], Kf[1], sb1[:], start=False, stop=True), inc="t")
        yield

        E.wait("vector", "t", nt_ks)
        if g.get("r_t") is not None:
            E.wait("vector", "t", g["r_t"])
        nv_r = E.op("vector", lambda v, nblcol=nblcol, vbetaO=vbetaO:
                    v.scalar_tensor_tensor(
                        rsb[:], bankP[:], nblcol, vbetaO,
                        ALU.mult, ALU.add), inc="v")
        E.op("vector", lambda v: v.drain())
        yield

        E.wait("tensor", "v", nv_r)
        if g.get("bq_a") is not None:
            E.wait("tensor", "a", g["bq_a"])
        if g.get("bq_v") is not None:
            E.wait("tensor", "v", g["bq_v"])
        nt_w = E.op("tensor", lambda t, MsbO=MsbO: t.matmul(
            bankQ[:], MsbO, rsb[:], start=True, stop=True), inc="t")
        g["r_t"] = nt_w
        E.wait("scalar", "t", nt_w)
        if g.get("w_t") is not None:
            E.wait("scalar", "t", g["w_t"])
        na_w = E.op("scalar", lambda a: a.copy(wsb[:], bankQ[:]), inc="a")
        E.op("scalar", lambda a: a.drain())
        g["bq_a"] = na_w
        if pend:
            pend.pop(0)()
        yield

        E.wait("tensor", "a", na_w)
        E.op("tensor", lambda t, PstO=PstO: t.matmul(
            bankP[:], PstO, wsb[:], start=True, stop=False))
        E.op("tensor", lambda t, Qf=Qf: t.matmul(
            bankP[:], Qf[0], sb0[:], start=False, stop=False))
        nt_o = E.op("tensor", lambda t, Qf=Qf: t.matmul(
            bankP[:], Qf[1], sb1[:], start=False, stop=True), inc="t")
        nt_ds0 = E.op("tensor", lambda t, kbarO=kbarO: t.matmul(
            bankQ[:], kbarO[:, 0:128], wsb[:], start=True, stop=True), inc="t")
        g["qkvc_t"] = nt_o
        yield

        E.wait("vector", "t", nt_ds0)
        nv_s0 = E.op("vector", lambda v, j=j: v.scalar_tensor_tensor(
            sf0[:], sf0[:], cerep[:, j:j + 1], bankQ[:], ALU.mult, ALU.add),
            inc="v")
        E.op("vector", lambda v: v.drain())
        E.wait("vector", "t", nt_o)
        if g.get("ocp_a") is not None:
            E.wait("vector", "a", g["ocp_a"])
        nv_oc = E.op("vector", lambda v: v.tensor_copy(ocp[:], bankP[:]),
                     inc="v")
        E.op("vector", lambda v: v.drain())
        g["bp_v"] = nv_oc
        yield

        E.wait("tensor", "v", nv_s0)
        E.wait("scalar", "t", nt_o)
        na_sq2 = E.op("scalar", lambda a, mscol=mscol: a.activation(
            oscr[:], bankP[:], AF.Square, accum_out=mscol), inc="a")
        E.op("scalar", lambda a: a.drain())
        nt_ds1 = E.op("tensor", lambda t, kbarO=kbarO: t.matmul(
            bankQ[:], kbarO[:, 128:256], wsb[:], start=True, stop=True),
            inc="t")
        g["w_t"] = nt_ds1
        g["ocp_a"] = na_sq2
        g["bp_a"] = max(g.get("bp_a") or 0, na_sq2)
        yield

        E.wait("vector", "t", nt_ds1)
        nv_s1 = E.op("vector", lambda v, j=j: v.scalar_tensor_tensor(
            sf1[:], sf1[:], cerep[:, j:j + 1], bankQ[:], ALU.mult, ALU.add),
            inc="v")
        E.op("vector", lambda v: v.drain())
        g["bq_v"] = nv_s1
        g["cer_v"] = max(g.get("cer_v") or 0, nv_s1)
        E.wait("scalar", "v", nv_s1)
        E.op("scalar", lambda a: a.copy(sb0[:], sf0[:]))
        na_sb = E.op("scalar", lambda a: a.copy(sb1[:], sf1[:]), inc="a")
        E.op("scalar", lambda a: a.drain())
        g["state_a"] = na_sb
        yield

        def epi(j=j, na_sq2=na_sq2, mscol=mscol, rscol=rscol):
            E.wait("vector", "a", na_sq2)
            E.op("vector", lambda v: v.tensor_scalar(
                mscol, mscol, 1.0 / DV, GN_EPS, ALU.mult, ALU.add))
            nv_ms = E.op("vector", lambda v: v.drain(), inc="v")
            E.wait("scalar", "v", nv_ms)
            E.op("scalar", lambda a: a.activation(rscol, mscol, AF.Ln))
            E.op("scalar", lambda a: a.drain())
            na_rs = E.op("scalar", lambda a: a.activation(
                rscol, rscol, AF.Exp, scale=-0.5), inc="a")
            E.op("scalar", lambda a: a.drain())
            g["cer_a"] = max(g.get("cer_a") or 0, na_rs)
            E.wait("vector", "a", na_rs)
            if g.get("osb_tok") is not None:
                E.wait_tok("vector", g["osb_tok"])
            nv_o = E.op("vector", lambda v: v.scalar_tensor_tensor(
                osb[:], ocp[:], rscol, gate_sb[:, j * 512:(j + 1) * 512],
                ALU.mult, ALU.mult), inc="v")
            E.op("vector", lambda v: v.drain())
            g["gate_v"] = nv_o
            E.wait("gpsimd", "v", nv_o)
            row0 = j * 256 + b * 128
            g["osb_tok"] = E.dma("gpsimd", oA_in[u][row0:row0 + 128, :], osb[:])
        pend.append(epi)
        yield
    for epi in pend:
        epi()
    g["pm_t"] = E.sems["t"].n
    return


def emit_p3(ctx):
    nc, E = ctx["nc"], ctx["E"]
    oA_out, out_sh = ctx["oA_out"], ctx["out_sh"]
    u_scr, ln2_scr, x3_scr = ctx["u_scr"], ctx["ln2_scr"], ctx["x3_scr"]

    with contextlib.ExitStack() as _st:
        owT = _st.enter_context(nc.sbuf_tensor("p3_owT", [128, 32 * 1024], BF16))
        oT = _st.enter_context(nc.sbuf_tensor("p3_oT", [128, 2 * 32 * 128], BF16))
        utile2 = _st.enter_context(nc.sbuf_tensor("p3_u", [128, 2 * D], F32))
        l22 = _st.enter_context(nc.sbuf_tensor("p3_l2", [128, 4], F32))
        ztile2 = _st.enter_context(nc.sbuf_tensor("p3_z", [128, 2 * D], F32))
        x3t2 = _st.enter_context(nc.sbuf_tensor("p3_x3", [128, 2 * D], F32))
        ps = _st.enter_context(nc.psum_tensor("p3_ps", [128, 1024], F32))
        E.barrier()
        for t in ctx["tok_owsc"]:
            E.wait_tok("sync", t)
        tow = [E.dma("sync", owT[:, kf * 1024:(kf + 1) * 1024],
                     ctx["ow_sc"][:, kf * 128:(kf + 1) * 128], transpose=True)
               for kf in range(32)]
        for t in tow:
            E.wait_tok("tensor", t)

        def load_oT(tt):
            b, uu = divmod(tt, NU)
            buf = (tt % 2) * 4096
            E.wait("sync", "cc", ctx["n_a2a"][uu])
            E.wait("scalar", "cc", ctx["n_a2a"][uu])
            tks = []
            for i in range(H):
                for fb in range(4):
                    r0 = i * 256 + b * 128
                    tks.append(E.dma(
                        "sync",
                        oT[:, buf + (i * 4 + fb) * 128:buf + (i * 4 + fb + 1) * 128],
                        oA_out[uu][r0:r0 + 128, fb * 128:(fb + 1) * 128],
                        transpose=True))
            return tks

        tks = load_oT(0)
        nt_prev = None
        nv_x3p = [None, None]
        t_x3p = [None, None]
        for tt in range(NTT):
            buf = (tt % 2) * 4096
            if tt + 1 < NTT:
                if nt_prev is not None:
                    E.wait("sync", "t", nt_prev)
                    E.wait("scalar", "t", nt_prev)
                tks_next = load_oT(tt + 1)
            sb = tt % 2
            utile = utile2[:, sb * D:(sb + 1) * D]
            ztile = ztile2[:, sb * D:(sb + 1) * D]
            x3t = x3t2[:, sb * D:(sb + 1) * D]
            l2 = l22[:, sb * 2:(sb + 1) * 2]
            if nv_x3p[sb] is not None:
                E.wait("gpsimd", "v", nv_x3p[sb])
            t_u = E.dma("gpsimd", utile[:], u_scr[tt * 128:(tt + 1) * 128, :])
            t_l2 = E.dma("gpsimd", l2[:], ln2_scr[tt * 128:(tt + 1) * 128, :])
            E.wait("tensor", "v", E.sems["v"].n)
            for kf in range(32):
                if kf < len(tks):
                    E.wait_tok("tensor", tks[kf])
                for hn in range(2):
                    nt = E.op("tensor", lambda t, kf=kf, hn=hn, buf=buf: t.matmul(
                        ps[:, hn * 512:hn * 512 + 512],
                        oT[:, buf + kf * 128:buf + (kf + 1) * 128],
                        owT[:, kf * 1024 + hn * 512:kf * 1024 + hn * 512 + 512],
                        start=(kf == 0), stop=(kf == 31)),
                        inc=("t" if kf == 31 and hn == 1 else None))
            nt_prev = nt
            if tt + 1 < NTT:
                tks = tks_next
            E.wait_tok("vector", t_u)
            E.wait_tok("vector", t_l2)
            E.op("vector", lambda v, ztile=ztile, utile=utile, l2=l2: v.tensor_scalar(
                ztile[:], utile[:], l2[:, 0:1], l2[:, 1:2], ALU.subtract, ALU.mult))
            E.op("vector", lambda v: v.drain())
            E.op("vector", lambda v, ztile=ztile, utile=utile: v.tensor_add(
                ztile[:], ztile[:], utile[:]))
            E.op("vector", lambda v: v.drain())
            if t_x3p[sb] is not None:
                E.wait_tok("vector", t_x3p[sb])
            E.wait("vector", "t", nt)
            nv_x3 = E.op("vector", lambda v, x3t=x3t, ztile=ztile: v.tensor_add(
                x3t[:], ztile[:], ps[:, 0:1024]), inc="v")
            E.op("vector", lambda v: v.drain())
            nv_x3p[sb] = nv_x3
            E.wait("gpsimd", "v", nv_x3)
            t_x3p[sb] = E.dma("gpsimd", x3_scr[tt * 128:(tt + 1) * 128, :], x3t[:])

    with contextlib.ExitStack() as _st:
        xb = _st.enter_context(nc.sbuf_tensor("p3_x", [128, 2 * D], F32))
        xn1 = _st.enter_context(nc.sbuf_tensor("p3_xn1", [128, D], BF16))
        scr = _st.enter_context(nc.sbuf_tensor("p3_scr", [128, D], F32))
        xn1T = _st.enter_context(nc.sbuf_tensor("p3_xn1T", [128, NK * U], BF16))
        wgT = _st.enter_context(nc.sbuf_tensor("p3_wgT", [128, NK * FH], BF16))
        wuT = _st.enter_context(nc.sbuf_tensor("p3_wuT", [128, NK * FH], BF16))
        hffT = _st.enter_context(nc.sbuf_tensor("p3_hffT", [128, NM * U], BF16))
        wdTk = _st.enter_context(nc.sbuf_tensor("p3_wdTk", [128, 2 * D], BF16))
        stats = _st.enter_context(nc.sbuf_tensor("p3_stats", [128, 48], F32))
        sgate = _st.enter_context(nc.sbuf_tensor("p3_sg", [128, U], BF16))
        otile = _st.enter_context(nc.sbuf_tensor("p3_out", [128, 2 * D], F32))
        ps_big = _st.enter_context(nc.psum_tensor("p3_ps_big", [128, 2048], F32))
        ps_up2 = _st.enter_context(nc.psum_tensor("p3_ps_up2", [128, 1024], F32))
        ps_t = _st.enter_context(nc.psum_tensor("p3_ps_t", [128, 512], BF16))
        E.barrier()
        ng0 = E.op("gpsimd", lambda g: g.memset(stats[:], 0.0), inc="g")
        E.wait("vector", "g", ng0)
        E.wait("scalar", "g", ng0)
        wu_toks = []
        for k in range(NK):
            wu_toks.append(E.dma("sync", wgT[:, k * FH:(k + 1) * FH],
                  ctx["wg_bf"][:, k * 128:(k + 1) * 128], transpose=True))
            wu_toks.append(E.dma("sync", wuT[:, k * FH:(k + 1) * FH],
                          ctx["wu_bf"][:, k * 128:(k + 1) * 128], transpose=True))

        tok_xb = [None, None]
        rd_xb = [0, 0]
        rd_xb_a = [0, 0]
        for half in range(2):
            hoff = half * 8
            E.barrier()
            if half == 1:
                wu_toks = []
                for k in range(NK):
                    wu_toks.append(E.dma("sync", wuT[:, k * FH:(k + 1) * FH],
                                   ctx["wu_bf"][:, k * 128:(k + 1) * 128],
                                   transpose=True))
            chain_v = [0, 0]
            stA = {}

            def finishB(i):
                sl = i % 2
                nv0, na0 = stA[i]
                n_rstd = emit_ln_stats(ctx, stats, nv0, na0, D, LN_EPS,
                                       only=sl, war=False)
                chain_v[sl] = E.sems["v"].n
                E.wait("vector", "a", n_rstd)
                nv = E.op("vector", lambda v, sl=sl: v.tensor_scalar(
                    xn1[:], xb[:, sl * D:(sl + 1) * D],
                    stats[:, 16 + sl:17 + sl], stats[:, 40 + sl:41 + sl],
                    ALU.subtract, ALU.mult), inc="v")
                rd_xb[sl] = nv
                E.wait("sync", "v", nv)
                emit_transpose_row(ctx, src=xn1, dst=xn1T, dst_cols=[
                    (k * U + i * 128) for k in range(NK)], nv=nv, ps_t=ps_t)

            for i in range(8):
                sl = i % 2
                if tok_xb[sl] is not None:
                    E.wait_tok("sync", tok_xb[sl])
                if rd_xb[sl]:
                    E.wait("sync", "v", rd_xb[sl])
                if rd_xb_a[sl]:
                    E.wait("sync", "a", rd_xb_a[sl])
                n_ldx = E.dma("sync", xb[:, sl * D:(sl + 1) * D],
                              x3_scr[(hoff + i) * 128:(hoff + i + 1) * 128, :])
                tok_xb[sl] = n_ldx
                E.wait_tok("vector", n_ldx)
                E.op("vector", lambda v, sl=sl: v.reduce_sum(
                    stats[:, sl:sl + 1], xb[:, sl * D:(sl + 1) * D], axis=AX))
                nv0 = E.op("vector", lambda v: v.drain(), inc="v")
                E.wait_tok("scalar", n_ldx)
                E.wait("scalar", "v", chain_v[sl])
                na0 = E.op("scalar", lambda a, sl=sl: a.activation(
                    scr[:], xb[:, sl * D:(sl + 1) * D], AF.Square,
                    accum_out=stats[:, 8 + sl:9 + sl]), inc="a")
                E.op("scalar", lambda a: a.drain())
                rd_xb_a[sl] = na0
                stA[i] = (nv0, na0)
                if i > 0:
                    finishB(i - 1)
            finishB(7)
            for _t in wu_toks:
                E.wait_tok("tensor", _t)
            E.wait("tensor", "a", E.sems["a"].n)
            na_sil = None
            nv_last = None
            nv_up = [None, None]
            for m in range(NM):
                upps = ps_big[:, 1024:2048] if m % 2 == 0 else ps_up2[:, 0:1024]
                if na_sil is not None:
                    E.wait("tensor", "a", na_sil)
                for hn in range(2):
                    for k in range(NK):
                        E.op("tensor", lambda t, m=m, k=k, hn=hn: t.matmul(
                            ps_big[:, hn * 512:hn * 512 + 512],
                            wgT[:, k * FH + m * 128:k * FH + (m + 1) * 128],
                            xn1T[:, k * U + hn * 512:k * U + hn * 512 + 512],
                            start=(k == 0), stop=(k == NK - 1)))
                if nv_up[m % 2] is not None:
                    E.wait("tensor", "v", nv_up[m % 2])
                for hn in range(2):
                    for k in range(NK):
                        ntl = E.op("tensor", lambda t, m=m, k=k, hn=hn, upps=upps: t.matmul(
                            upps[:, hn * 512:hn * 512 + 512],
                            wuT[:, k * FH + m * 128:k * FH + (m + 1) * 128],
                            xn1T[:, k * U + hn * 512:k * U + hn * 512 + 512],
                            start=(k == 0), stop=(k == NK - 1)), inc="t")
                nt = ntl
                E.wait("scalar", "t", nt)
                if nv_last is not None:
                    E.wait("scalar", "v", nv_last)
                E.op("scalar", lambda a: a.activation(sgate[:], ps_big[:, 0:U],
                                                      AF.Silu, scale=QS))
                na_sil = E.op("scalar", lambda a: a.drain(), inc="a")
                E.wait("vector", "a", na_sil)
                nv_up[m % 2] = E.op("vector", lambda v, m=m, upps=upps: v.scalar_tensor_tensor(
                    hffT[:, m * U:(m + 1) * U], upps[:], QS, sgate[:],
                    ALU.mult, ALU.mult), inc="v")
                nv_last = nv_up[m % 2]
            wd_toks = []
            E.wait("sync", "t", E.sems["t"].n)
            E.wait("scalar", "t", E.sems["t"].n)
            for k in range(NM):
                wd_toks.append(E.dma("sync", wuT[:, k * 1024:(k + 1) * 1024],
                                     ctx["wd_bf"][:, k * 128:(k + 1) * 128],
                                     transpose=True))
            E.wait("tensor", "a", E.sems["a"].n)
            for p in range(4):
                i0, i1 = 2 * p, 2 * p + 1
                E.wait("tensor", "v", E.sems["v"].n)
                for k in range(NM):
                    if p == 0:
                        E.wait_tok("tensor", wd_toks[k])
                    for jj, i in ((0, i0), (1, i1)):
                        for hn in range(2):
                            ntl = E.op("tensor", lambda t, k=k, jj=jj, i=i, hn=hn: t.matmul(
                                ps_big[:, jj * 1024 + hn * 512:jj * 1024 + hn * 512 + 512],
                                hffT[:, k * U + i * 128:k * U + (i + 1) * 128],
                                wuT[:, k * 1024 + hn * 512:k * 1024 + hn * 512 + 512],
                                start=(k == 0), stop=(k == NM - 1)), inc="t")
                nt_d = ntl
                for jj, i in ((0, i0), (1, i1)):
                    if tok_xb[jj] is not None:
                        E.wait_tok("sync", tok_xb[jj])
                    if rd_xb[jj]:
                        E.wait("sync", "v", rd_xb[jj])
                    n_rx = E.dma("sync", xb[:, jj * D:(jj + 1) * D],
                                 x3_scr[(hoff + i) * 128:(hoff + i + 1) * 128, :])
                    tok_xb[jj] = n_rx
                    E.wait("vector", "t", nt_d)
                    E.wait_tok("vector", n_rx)
                    nv_u = E.op("vector", lambda v, jj=jj: v.scalar_tensor_tensor(
                        otile[:, jj * D:(jj + 1) * D],
                        ps_big[:, jj * 1024:(jj + 1) * 1024],
                        0.5 * QS, xb[:, jj * D:(jj + 1) * D],
                        ALU.mult, ALU.add), inc="v")
                    rd_xb[jj] = nv_u
                    E.op("vector", lambda v: v.drain())
                    E.wait("gpsimd", "v", nv_u)
                    t_o = E.dma("gpsimd", out_sh[(hoff + i) * 128:(hoff + i + 1) * 128, :],
                                otile[:, jj * D:(jj + 1) * D])
                    E.wait_tok("vector", t_o)



_NC_CACHE = {}


def _get_nc():
    if DEBUG not in _NC_CACHE:
        _NC_CACHE[DEBUG] = build_program()
    return _NC_CACHE[DEBUG]


def shard_inputs(inputs):
    x = np.asarray(inputs["L0_attn_x"])
    q_w = np.asarray(inputs["L0_attn_q_w"])
    k_w = np.asarray(inputs["L0_attn_k_w"])
    v_w = np.asarray(inputs["L0_attn_v_w"])
    g_w = np.asarray(inputs["L0_attn_g_w"])
    a_w = np.asarray(inputs["L0_attn_a_w"])
    b_w = np.asarray(inputs["L0_attn_b_w"])
    A_log = np.asarray(inputs["L0_attn_A_log"])
    dt_b = np.asarray(inputs["L0_attn_dt_bias"])
    q_c = np.asarray(inputs["L0_attn_q_conv"])
    k_c = np.asarray(inputs["L0_attn_k_conv"])
    v_c = np.asarray(inputs["L0_attn_v_conv"])
    norm_w = np.asarray(inputs["L0_attn_norm_w"])
    o_norm = np.asarray(inputs["L0_attn_o_norm_w"])
    o_w = np.asarray(inputs["L0_attn_o_w"])
    wg = np.asarray(inputs["L0_ffn_w_gate"])
    wu = np.asarray(inputs["L0_ffn_w_up"])
    wd = np.asarray(inputs["L0_ffn_w_down"])

    in_maps = []
    for c in range(NCORE):
        xs = np.empty((LT, D), np.float32)
        for tt in range(NTT):
            b, u = divmod(tt, NU)
            xs[tt * 128:(tt + 1) * 128] = x[b, u * U + c * 128: u * U + c * 128 + 128]
        qs = slice(c * DK, (c + 1) * DK)
        vs = slice(c * DV, (c + 1) * DV)
        wh = np.concatenate([
            q_w[qs], k_w[qs], v_w[vs], g_w[vs],
            a_w[c:c + 1], np.zeros((127, D), np.float32),
            b_w[c:c + 1], np.zeros((127, D), np.float32),
        ], 0)
        cw = np.concatenate([q_c[qs], k_c[qs], v_c[vs]], 0)
        cw = cw.reshape(8, 128, 4).transpose(1, 0, 2).reshape(128, 32)
        in_maps.append({
            "x_sh": np.ascontiguousarray(xs),
            "w_gate": wg, "w_up": wu, "w_down": wd,
            "norm_w": norm_w.reshape(1, D).astype(np.float32),
            "w_head": np.ascontiguousarray(wh.astype(np.float32)),
            "conv_w": np.ascontiguousarray(cw.astype(np.float32)),
            "o_w": o_w,
            "o_norm": np.tile(o_norm, H).reshape(1, H * DV).astype(np.float32),
            "adt": np.array([[A_log[c], dt_b[c]]], np.float32),
        })
    return in_maps


def unshard_output(results):
    out = np.empty((B, T, D), np.float32)
    for c in range(NCORE):
        o = results[c]["out_sh"]
        for tt in range(NTT):
            b, u = divmod(tt, NU)
            out[b, u * U + c * 128: u * U + c * 128 + 128] = o[tt * 128:(tt + 1) * 128]
    return out


def kernel(**inputs):
    in_maps = shard_inputs(inputs)
    nc = _get_nc()
    res = run_bass_kernel_spmd(nc, in_maps, core_ids=list(range(NCORE)))
    return unshard_output(res.results)


# version 10 (final)
# speedup vs baseline: 1.0106x; 1.0106x over previous
import contextlib
import numpy as np
import concourse.bass as bass
import concourse.mybir as mybir
from concourse.aot_env import allow_ici_all_to_all
from concourse.bass_utils import run_bass_kernel_spmd

F32 = mybir.dt.float32
BF16 = mybir.dt.bfloat16
I32 = mybir.dt.int32
AF = mybir.ActivationFunctionType
ALU = mybir.AluOpType
AX = mybir.AxisListType.X

B, T, D = 2, 8192, 1024
H, DK, DV = 8, 256, 512
FH = 3072
QS = 1.0 / 127.0
NCORE = 8
C = 128
U = 1024
NU = T // U
LT = B * T // NCORE
NTT = LT // 128
NK = D // 128
NM = FH // 128
LN_EPS = 1e-5
RMS_EPS = 1e-6
GN_EPS = 1e-5
QKN_EPS = 1e-6

import os
DEBUG = 0
TRUNC = int(os.environ.get("KTRUNC", "99"))
P2U = int(os.environ.get("KP2U", "99"))
P2J = int(os.environ.get("KP2J", "99"))
P2CC = int(os.environ.get("KP2CC", "1"))
B1W = int(os.environ.get("KB1W", "1"))


class Sem:
    def __init__(self, handle):
        self.h = handle
        self.n = 0

    def bump(self, k):
        self.n += k
        return self.n


class Emit:

    ENGS = ("sync", "vector", "scalar", "tensor", "gpsimd")
    RING = 6

    def __init__(self):
        self.code = {k: [] for k in self.ENGS}
        self.sems = {}
        self.rings = {}
        self.ridx = {}

    def raw(self, eng, fn):
        self.code[eng].append(fn)

    def wait(self, eng, sem, val):
        s = self.sems[sem]
        self.code[eng].append(lambda e, s=s, val=val: e.wait_ge(s.h, val))

    def wait_tok(self, eng, tok):
        s, n = tok
        self.code[eng].append(lambda e, s=s, n=n: e.wait_ge(s.h, n))

    def op(self, eng, fn, inc=None, k=1):
        if inc is None:
            self.code[eng].append(fn)
            return None
        s = self.sems[inc]
        n = s.bump(k)
        self.code[eng].append(lambda e, fn=fn, s=s, k=k: fn(e).then_inc(s.h, k))
        return n

    def dma(self, eng, out, in_, sem=None, **kw):
        q = sem or ("gdma" if eng == "gpsimd" else "dma")
        ring = self.rings[q]
        i = self.ridx[q] % len(ring)
        self.ridx[q] += 1
        s = ring[i]
        if s.n > 0:
            self.code[eng].append(lambda e, s=s, n=s.n: e.wait_ge(s.h, n))
        n = s.bump(16)
        self.code[eng].append(
            lambda e, out=out, in_=in_, s=s, kw=kw: e.dma_start(out=out, in_=in_, **kw).then_inc(s.h, 16))
        return (s, n)

    def wait_queue(self, eng, q):
        for s in self.rings[q]:
            if s.n > 0:
                self.code[eng].append(lambda e, s=s, n=s.n: e.wait_ge(s.h, n))

    def barrier(self, engines=None):
        snap = [(s, s.n) for s in self.sems.values() if s.n > 0]
        for eng in engines or self.ENGS:
            for s, n in snap:
                self.code[eng].append(lambda e, s=s, n=n: e.wait_ge(s.h, n))

    def run(self, eng_name, eng):
        for fn in self.code[eng_name]:
            fn(eng)


def build_program():
    nc = bass.Bass("TRN2", target_bir_lowering=False, debug=False)

    x_sh = nc.dram_tensor("x_sh", [LT, D], F32, kind="ExternalInput").ap()
    w_gate = nc.dram_tensor("w_gate", [FH, D], I32, kind="ExternalInput").ap()
    w_up = nc.dram_tensor("w_up", [FH, D], I32, kind="ExternalInput").ap()
    w_down = nc.dram_tensor("w_down", [D, FH], I32, kind="ExternalInput").ap()
    norm_w = nc.dram_tensor("norm_w", [1, D], F32, kind="ExternalInput").ap()
    w_head = nc.dram_tensor("w_head", [1792, D], F32, kind="ExternalInput").ap()
    conv_w = nc.dram_tensor("conv_w", [128, 32], F32, kind="ExternalInput").ap()
    o_w = nc.dram_tensor("o_w", [D, H * DV], F32, kind="ExternalInput").ap()
    o_norm = nc.dram_tensor("o_norm", [1, H * DV], F32, kind="ExternalInput").ap()
    adt = nc.dram_tensor("adt", [1, 2], F32, kind="ExternalInput").ap()

    out_sh = nc.dram_tensor("out_sh", [LT, D], F32, kind="ExternalOutput").ap()

    wg_bf = nc.dram_tensor("wg_bf", [FH, D], BF16, kind="Internal").ap()
    wu_bf = nc.dram_tensor("wu_bf", [FH, D], BF16, kind="Internal").ap()
    wd_bf = nc.dram_tensor("wd_bf", [D, FH], BF16, kind="Internal").ap()
    wh_bf = nc.dram_tensor("wh_bf", [1792, D], BF16, kind="Internal").ap()
    ow_bf = nc.dram_tensor("ow_bf", [D, H * DV], BF16, kind="Internal").ap()
    ow_sc = nc.dram_tensor("ow_sc", [D, H * DV], BF16, kind="Internal").ap()
    u_scr = nc.dram_tensor("u_scr", [LT, D], F32, kind="Internal").ap()
    x3_scr = nc.dram_tensor("x3_scr", [LT, D], F32, kind="Internal").ap()
    ln2_scr = nc.dram_tensor("ln2_scr", [LT, 2], F32, kind="Internal").ap()
    hT_sh = nc.dram_tensor("hT_sh", [D, LT], BF16, kind="Internal").ap()
    hT_full = nc.dram_tensor("hT_full", [NCORE * D, LT], BF16, kind="Internal").ap()
    oA_in = [nc.dram_tensor(f"oA_in{u}", [2048, DV], BF16, kind="Internal").ap()
             for u in range(NU)]
    oA_out = [nc.dram_tensor(f"oA_out{u}", [2048, DV], BF16, kind="Internal").ap()
              for u in range(NU)]

    dbg = {}
    if DEBUG == 1:
        dbg["u_dbg"] = nc.dram_tensor("u_dbg", [LT, D], F32, kind="ExternalOutput").ap()
        dbg["h_dbg"] = nc.dram_tensor("h_dbg", [NCORE * D, LT], BF16, kind="ExternalOutput").ap()
        dbg["xn1T_dbg"] = nc.dram_tensor("xn1T_dbg", [128, NK * U], BF16, kind="ExternalOutput").ap()
        dbg["xn1_dbg"] = nc.dram_tensor("xn1_dbg", [LT, D], BF16, kind="ExternalOutput").ap()
        dbg["st_dbg"] = nc.dram_tensor("st_dbg", [NTT * 128, 48], F32, kind="ExternalOutput").ap()
        dbg["hffT_dbg"] = nc.dram_tensor("hffT_dbg", [128, NM * U], BF16, kind="ExternalOutput").ap()
    if DEBUG == 2:
        dbg["o_dbg"] = nc.dram_tensor("o_dbg", [NU * 2048, DV], BF16, kind="ExternalOutput").ap()

    E = Emit()

    with (
        nc.sbuf_tensor("ident_bf", [128, 128], BF16) as ident_bf,
        nc.sbuf_tensor("ident_f32", [128, 128], F32) as ident_f32,
        nc.sbuf_tensor("ones_row", [1, 128], BF16) as ones_row,
        nc.sbuf_tensor("onesr_f32", [1, 128], F32) as onesr_f32,
        nc.sbuf_tensor("onesc_bf", [128, 1], BF16) as onesc_bf,
        nc.sbuf_tensor("onec_f32", [1, 1], F32) as onec_f32,
        nc.sbuf_tensor("mask_st", [128, 128], F32) as mask_st,
        nc.sbuf_tensor("mask_ts", [128, 128], F32) as mask_ts,
        nc.sbuf_tensor("normw_rep", [128, D], F32) as normw_rep,
        nc.sbuf_tensor("convw", [128, 32], F32) as convw,
        nc.sbuf_tensor("adt_sb", [1, 2], F32) as adt_sb,
        nc.sbuf_tensor("negexpA", [1, 1], F32) as negexpA,
        nc.semaphore("s_t") as _s4,
        nc.semaphore("s_v") as _s5,
        nc.semaphore("s_a") as _s6,
        nc.semaphore("s_g") as _s7,
        nc.semaphore("s_cc") as _s8,
        contextlib.ExitStack() as _stk,
        nc.Block() as block,
    ):
        for nm, h in [("t", _s4), ("v", _s5), ("a", _s6), ("g", _s7), ("cc", _s8)]:
            E.sems[nm] = Sem(h)
        for q in ("dma", "gdma", "adma"):
            E.rings[q] = []
            E.ridx[q] = 0
            for r in range(Emit.RING):
                h = _stk.enter_context(nc.semaphore(f"s_{q}{r}"))
                s = Sem(h)
                E.rings[q].append(s)
                E.sems[f"{q}{r}"] = s

        ctx = dict(
            nc=nc, E=E, ident_bf=ident_bf, ident_f32=ident_f32,
            ones_row=ones_row, onesr_f32=onesr_f32, onesc_bf=onesc_bf,
            onec_f32=onec_f32, mask_st=mask_st, mask_ts=mask_ts,
            normw_rep=normw_rep, convw=convw, adt_sb=adt_sb, negexpA=negexpA,
            x_sh=x_sh, w_head=w_head, conv_w=conv_w, o_w=o_w, o_norm=o_norm,
            adt=adt, norm_w=norm_w,
            wg_bf=wg_bf, wu_bf=wu_bf, wd_bf=wd_bf, wh_bf=wh_bf, ow_bf=ow_bf,
            ow_sc=ow_sc, u_scr=u_scr, x3_scr=x3_scr, ln2_scr=ln2_scr, hT_sh=hT_sh,
            hT_full=hT_full, oA_in=oA_in, oA_out=oA_out, out_sh=out_sh,
            w_gate=w_gate, w_up=w_up, w_down=w_down, dbg=dbg,
        )

        emit_prologue(ctx)
        emit_p1(ctx)

        E.barrier(["gpsimd"])
        if TRUNC >= 50:
            n_ag1 = E.op("gpsimd", lambda g: g.collective_compute(
                "AllGather", ALU.bypass, replica_groups=[list(range(NCORE))],
                ins=[hT_sh], outs=[hT_full]), inc="cc")
            ctx["n_ag1"] = n_ag1

        if DEBUG == 1:
            if TRUNC >= 50:
                E.wait("gpsimd", "cc", ctx["n_ag1"])
                E.dma("gpsimd", dbg["h_dbg"], hT_full)
            E.wait_queue("gpsimd", "gdma")
            E.dma("gpsimd", out_sh, u_scr)
            E.dma("gpsimd", dbg["u_dbg"], u_scr) if TRUNC < 40 else None
            E.wait_queue("gpsimd", "gdma")
        else:
            emit_p2(ctx)
            if DEBUG == 2:
                E.wait("gpsimd", "cc", ctx["n_a2a"][NU - 1])
                for u in range(NU):
                    E.dma("gpsimd", dbg["o_dbg"][u * 2048:(u + 1) * 2048, :], oA_out[u])
                E.wait_queue("gpsimd", "gdma")
                E.dma("gpsimd", out_sh, u_scr)
                E.wait_queue("gpsimd", "gdma")
            else:
                emit_p3(ctx)
                E.barrier()

        @block.sync
        def _(sync):
            E.run("sync", sync)

        @block.vector
        def _(vector):
            E.run("vector", vector)

        @block.scalar
        def _(scalar):
            E.run("scalar", scalar)

        @block.tensor
        def _(tensor):
            E.run("tensor", tensor)

        @block.gpsimd
        def _(gpsimd):
            E.run("gpsimd", gpsimd)

    return nc


def emit_prologue(ctx):
    nc, E = ctx["nc"], ctx["E"]

    n1 = E.op("gpsimd", lambda g: g.memset(ctx["ident_bf"][:], 0.0), inc="g")
    n1b = E.op("gpsimd", lambda g: g.memset(ctx["ident_f32"][:], 0.0), inc="g")
    E.wait("gpsimd", "g", n1b)
    E.op("gpsimd", lambda g: g.affine_select(
        out=ctx["ident_bf"][:], in_=ctx["ident_bf"][:],
        compare_op=ALU.not_equal, fill=1.0, base=0,
        pattern=[[-1, 128]], channel_multiplier=1))
    E.op("gpsimd", lambda g: g.affine_select(
        out=ctx["ident_f32"][:], in_=ctx["ident_f32"][:],
        compare_op=ALU.not_equal, fill=1.0, base=0,
        pattern=[[-1, 128]], channel_multiplier=1))
    E.op("gpsimd", lambda g: g.memset(ctx["mask_st"][:], 0.0))
    E.op("gpsimd", lambda g: g.memset(ctx["mask_ts"][:], 0.0))
    ng_mm = E.op("gpsimd", lambda g: g.memset(ctx["ones_row"][:], 1.0), inc="g")
    E.wait("gpsimd", "g", ng_mm)
    E.op("gpsimd", lambda g: g.affine_select(
        out=ctx["mask_st"][:], in_=ctx["mask_st"][:], compare_op=ALU.is_ge,
        fill=-1e30, base=0, pattern=[[1, 128]], channel_multiplier=-1))
    E.op("gpsimd", lambda g: g.affine_select(
        out=ctx["mask_ts"][:], in_=ctx["mask_ts"][:], compare_op=ALU.is_gt,
        fill=-1e30, base=0, pattern=[[-1, 128]], channel_multiplier=1))
    E.op("gpsimd", lambda g: g.memset(ctx["ones_row"][:], 1.0))
    E.op("gpsimd", lambda g: g.memset(ctx["onesr_f32"][:], 1.0))
    E.op("gpsimd", lambda g: g.memset(ctx["onesc_bf"][:], 1.0))
    ctx["n_consts"] = E.op("gpsimd", lambda g: g.memset(ctx["onec_f32"][:], 1.0), inc="g")

    ctx["tok_wg"] = E.dma("gpsimd", ctx["wg_bf"], ctx["w_gate"])
    ctx["tok_wu"] = E.dma("gpsimd", ctx["wu_bf"], ctx["w_up"])
    ctx["tok_wd"] = E.dma("gpsimd", ctx["wd_bf"], ctx["w_down"])
    ctx["tok_wh"] = E.dma("gpsimd", ctx["wh_bf"], ctx["w_head"])
    tok_ow = E.dma("gpsimd", ctx["ow_bf"], ctx["o_w"])
    ctx["tok_ow"] = tok_ow

    ctx["tok_normw"] = E.dma("sync", ctx["normw_rep"][:], ctx["norm_w"].broadcast_to([128, D]))
    tok_adt = E.dma("sync", ctx["adt_sb"][:], ctx["adt"])
    ctx["tok_convw"] = E.dma("sync", ctx["convw"][:, 0:32], ctx["conv_w"])

    E.wait_tok("scalar", tok_adt)
    E.op("scalar", lambda a: a.activation(ctx["negexpA"][:], ctx["adt_sb"][:, 0:1], AF.Exp))
    E.op("scalar", lambda a: a.drain())
    E.op("scalar", lambda a: a.mul(ctx["negexpA"][:], ctx["negexpA"][:], -1.0))
    ctx["n_negexpA"] = E.op("scalar", lambda a: a.drain(), inc="a")

    with nc.sbuf_tensor("pro_owt", [128, H * DV], BF16) as owt, \
         nc.sbuf_tensor("pro_onw", [128, H * DV], BF16) as onw:
        n_onw = E.dma("gpsimd", onw[:], ctx["o_norm"].broadcast_to([128, H * DV]))
        for i in range(8):
            if i == 0:
                E.wait_tok("sync", tok_ow)
            n_ld = E.dma("sync", owt[:], ctx["ow_bf"][i * 128:(i + 1) * 128, :])
            E.wait_tok("vector", n_ld)
            E.wait_tok("vector", n_onw)
            n_mul = E.op("vector", lambda v: v.tensor_mul(owt[:], owt[:], onw[:]), inc="v")
            E.wait("gpsimd", "v", n_mul)
            n_st = E.dma("gpsimd", ctx["ow_sc"][i * 128:(i + 1) * 128, :], owt[:])
            ctx.setdefault("tok_owsc", []).append(n_st)
            E.wait_tok("sync", n_st)
        


def emit_p1(ctx):
    nc, E = ctx["nc"], ctx["E"]
    x_sh, u_scr, ln2_scr, hT_sh = ctx["x_sh"], ctx["u_scr"], ctx["ln2_scr"], ctx["hT_sh"]

    with contextlib.ExitStack() as _st:
        xb = _st.enter_context(nc.sbuf_tensor("p1_x", [128, 2 * D], F32))
        xn1 = _st.enter_context(nc.sbuf_tensor("p1_xn1", [128, D], BF16))
        hstg = _st.enter_context(nc.sbuf_tensor("p1_hstg", [128, D], BF16))
        scr = _st.enter_context(nc.sbuf_tensor("p1_scr", [128, D], F32))
        xn1T = _st.enter_context(nc.sbuf_tensor("p1_xn1T", [128, NK * U], BF16))
        wgT = _st.enter_context(nc.sbuf_tensor("p1_wgT", [128, NK * FH], BF16))
        wuT = _st.enter_context(nc.sbuf_tensor("p1_wuT", [128, NK * FH], BF16))
        hffT = _st.enter_context(nc.sbuf_tensor("p1_hffT", [128, NM * U], BF16))
        wdTk = _st.enter_context(nc.sbuf_tensor("p1_wdTk", [128, 2 * D], BF16))
        stats = _st.enter_context(nc.sbuf_tensor("p1_stats", [128, 48], F32))
        sgate = _st.enter_context(nc.sbuf_tensor("p1_sg", [128, U], BF16))
        utile = _st.enter_context(nc.sbuf_tensor("p1_u", [128, 2 * D], F32))
        xn2 = _st.enter_context(nc.sbuf_tensor("p1_xn2", [128, D], BF16))
        htile = _st.enter_context(nc.sbuf_tensor("p1_h", [128, D], BF16))
        ps_big = _st.enter_context(nc.psum_tensor("p1_ps_big", [128, 2048], F32))
        ps_up2 = _st.enter_context(nc.psum_tensor("p1_ps_up2", [128, 1024], F32))
        ps_t = _st.enter_context(nc.psum_tensor("p1_ps_t", [128, 512], BF16))
        ng0 = E.op("gpsimd", lambda g: g.memset(stats[:], 0.0), inc="g")
        E.wait("vector", "g", ng0)
        E.wait("scalar", "g", ng0)
        if TRUNC < 1:
            return
        E.wait("tensor", "g", ctx["n_consts"])
        E.wait_tok("sync", ctx["tok_wg"])
        E.wait_tok("sync", ctx["tok_wu"])
        E.wait_tok("sync", ctx["tok_wd"])
        wu_toks = []
        for k in range(NK):
            wu_toks.append(E.dma("sync", wgT[:, k * FH:(k + 1) * FH],
                  ctx["wg_bf"][:, k * 128:(k + 1) * 128], transpose=True))
            wu_toks.append(E.dma("sync", wuT[:, k * FH:(k + 1) * FH],
                          ctx["wu_bf"][:, k * 128:(k + 1) * 128], transpose=True))

        if TRUNC < 2:
            return
        tok_xb = [None, None]
        rd_xb = [0, 0]
        rd_xb_a = [0, 0]
        for half in range(2):
            hoff = half * 8
            E.barrier()
            if half == 1:
                wu_toks = []
                for k in range(NK):
                    wu_toks.append(E.dma("sync", wuT[:, k * FH:(k + 1) * FH],
                                   ctx["wu_bf"][:, k * 128:(k + 1) * 128],
                                   transpose=True))

            chain_v = [0, 0]
            stA = {}

            def finishB(i):
                sl = i % 2
                nv0, na0 = stA[i]
                n_rstd = emit_ln_stats(ctx, stats, nv0, na0, D, LN_EPS,
                                       only=sl, war=False)
                chain_v[sl] = E.sems["v"].n
                if TRUNC < 21:
                    return
                E.wait("vector", "a", n_rstd)
                nv = E.op("vector", lambda v, sl=sl: v.tensor_scalar(
                    xn1[:], xb[:, sl * D:(sl + 1) * D],
                    stats[:, 16 + sl:17 + sl], stats[:, 40 + sl:41 + sl],
                    ALU.subtract, ALU.mult), inc="v")
                rd_xb[sl] = nv
                E.wait("sync", "v", nv)
                if DEBUG == 1:
                    E.wait("gpsimd", "v", nv)
                    t_x1 = E.dma("gpsimd", ctx["dbg"]["xn1_dbg"][(hoff + i) * 128:(hoff + i + 1) * 128, :], xn1[:])
                    n_sd = E.dma("gpsimd", ctx["dbg"]["st_dbg"][(hoff + i) * 128:(hoff + i + 1) * 128, :], stats[:])
                    E.wait_tok("vector", t_x1)
                    E.wait_tok("vector", n_sd)
                    E.wait_tok("scalar", n_sd)
                if TRUNC < 22:
                    return
                emit_transpose_row(ctx, src=xn1, dst=xn1T, dst_cols=[
                    (k * U + i * 128) for k in range(NK)], nv=nv, ps_t=ps_t)

            for i in range(8):
                sl = i % 2
                if tok_xb[sl] is not None:
                    E.wait_tok("sync", tok_xb[sl])
                if rd_xb[sl]:
                    E.wait("sync", "v", rd_xb[sl])
                if rd_xb_a[sl]:
                    E.wait("sync", "a", rd_xb_a[sl])
                n_ldx = E.dma("sync", xb[:, sl * D:(sl + 1) * D],
                              x_sh[(hoff + i) * 128:(hoff + i + 1) * 128, :])
                tok_xb[sl] = n_ldx
                E.wait_tok("vector", n_ldx)
                E.op("vector", lambda v, sl=sl: v.reduce_sum(
                    stats[:, sl:sl + 1], xb[:, sl * D:(sl + 1) * D], axis=AX))
                nv0 = E.op("vector", lambda v: v.drain(), inc="v")
                E.wait_tok("scalar", n_ldx)
                E.wait("scalar", "v", chain_v[sl])
                na0 = E.op("scalar", lambda a, sl=sl: a.activation(
                    scr[:], xb[:, sl * D:(sl + 1) * D], AF.Square,
                    accum_out=stats[:, 8 + sl:9 + sl]), inc="a")
                E.op("scalar", lambda a: a.drain())
                rd_xb_a[sl] = na0
                stA[i] = (nv0, na0)
                if i > 0:
                    finishB(i - 1)
            finishB(7)

            if DEBUG == 1 and half == 0:
                E.barrier(["gpsimd"])
                E.dma("gpsimd", ctx["dbg"]["xn1T_dbg"], xn1T[:])
            if TRUNC < 30:
                continue
            for _t in wu_toks:
                E.wait_tok("tensor", _t)
            E.wait("tensor", "a", E.sems["a"].n)
            na_sil = None
            nv_last = None
            nv_up = [None, None]
            for m in range(2 if TRUNC == 30 else NM):
                upps = ps_big[:, 1024:2048] if m % 2 == 0 else ps_up2[:, 0:1024]
                if na_sil is not None:
                    E.wait("tensor", "a", na_sil)
                for hn in range(2):
                    for k in range(NK):
                        E.op("tensor", lambda t, m=m, k=k, hn=hn: t.matmul(
                            ps_big[:, hn * 512:hn * 512 + 512],
                            wgT[:, k * FH + m * 128:k * FH + (m + 1) * 128],
                            xn1T[:, k * U + hn * 512:k * U + hn * 512 + 512],
                            start=(k == 0), stop=(k == NK - 1)))
                if nv_up[m % 2] is not None:
                    E.wait("tensor", "v", nv_up[m % 2])
                for hn in range(2):
                    for k in range(NK):
                        ntl = E.op("tensor", lambda t, m=m, k=k, hn=hn, upps=upps: t.matmul(
                            upps[:, hn * 512:hn * 512 + 512],
                            wuT[:, k * FH + m * 128:k * FH + (m + 1) * 128],
                            xn1T[:, k * U + hn * 512:k * U + hn * 512 + 512],
                            start=(k == 0), stop=(k == NK - 1)), inc="t")
                nt = ntl
                E.wait("scalar", "t", nt)
                if nv_last is not None:
                    E.wait("scalar", "v", nv_last)
                E.op("scalar", lambda a: a.activation(sgate[:], ps_big[:, 0:U],
                                                      AF.Silu, scale=QS))
                na_sil = E.op("scalar", lambda a: a.drain(), inc="a")
                E.wait("vector", "a", na_sil)
                nv_up[m % 2] = E.op("vector", lambda v, m=m, upps=upps: v.scalar_tensor_tensor(
                    hffT[:, m * U:(m + 1) * U], upps[:], QS, sgate[:],
                    ALU.mult, ALU.mult), inc="v")
                nv_last = nv_up[m % 2]

            if DEBUG == 1 and half == 0:
                E.barrier(["gpsimd"])
                E.dma("gpsimd", ctx["dbg"]["hffT_dbg"], hffT[:])
            if TRUNC < 40:
                continue
            wd_toks = []
            E.wait("sync", "t", E.sems["t"].n)
            E.wait("scalar", "t", E.sems["t"].n)
            for k in range(NM):
                wd_toks.append(E.dma("sync", wuT[:, k * 1024:(k + 1) * 1024],
                                     ctx["wd_bf"][:, k * 128:(k + 1) * 128],
                                     transpose=True))
            E.wait("tensor", "a", E.sems["a"].n)
            for p in range(4):
                i0, i1 = 2 * p, 2 * p + 1
                E.wait("tensor", "v", E.sems["v"].n)
                for k in range(NM):
                    if p == 0:
                        E.wait_tok("tensor", wd_toks[k])
                    for j, i in ((0, i0), (1, i1)):
                        for hn in range(2):
                            ntl = E.op("tensor", lambda t, k=k, j=j, i=i, hn=hn: t.matmul(
                                ps_big[:, j * 1024 + hn * 512:j * 1024 + hn * 512 + 512],
                                hffT[:, k * U + i * 128:k * U + (i + 1) * 128],
                                wuT[:, k * 1024 + hn * 512:k * 1024 + hn * 512 + 512],
                                start=(k == 0), stop=(k == NM - 1)), inc="t")
                nt_d = ntl
                for j, i in ((0, i0), (1, i1)):
                    if tok_xb[j] is not None:
                        E.wait_tok("sync", tok_xb[j])
                    if rd_xb[j]:
                        E.wait("sync", "v", rd_xb[j])
                    n_rx = E.dma("sync", xb[:, j * D:(j + 1) * D],
                                 x_sh[(hoff + i) * 128:(hoff + i + 1) * 128, :])
                    tok_xb[j] = n_rx
                    E.wait("vector", "t", nt_d)
                    E.wait_tok("vector", n_rx)
                    nv_u = E.op("vector", lambda v, j=j: v.scalar_tensor_tensor(
                        utile[:, j * D:(j + 1) * D], ps_big[:, j * 1024:(j + 1) * 1024],
                        0.5 * QS, xb[:, j * D:(j + 1) * D],
                        ALU.mult, ALU.add), inc="v")
                    rd_xb[j] = nv_u
                    E.wait("gpsimd", "v", nv_u)
                    E.dma("gpsimd", u_scr[(hoff + i) * 128:(hoff + i + 1) * 128, :],
                          utile[:, j * D:(j + 1) * D])
                    if DEBUG == 1:
                        E.dma("gpsimd",
                              ctx["dbg"]["u_dbg"][(hoff + i) * 128:(hoff + i + 1) * 128, :],
                              utile[:, j * D:(j + 1) * D])
                    E.op("vector", lambda v: v.drain())
                    E.op("vector", lambda v, j=j: v.reduce_sum(
                        stats[:, 0:1], utile[:, j * D:(j + 1) * D], axis=AX))
                    nv1 = E.op("vector", lambda v: v.drain(), inc="v")
                    E.wait("scalar", "v", nv_u)
                    E.wait("scalar", "v", E.sems["v"].n)
                    na1 = E.op("scalar", lambda a, j=j: a.activation(
                        scr[:], utile[:, j * D:(j + 1) * D], AF.Square,
                        accum_out=stats[:, 8:9]), inc="a")
                    E.op("scalar", lambda a: a.drain())
                    n_r2 = emit_ln_stats(ctx, stats, nv1, na1, D, LN_EPS, only=0)
                    E.wait("vector", "a", n_r2)
                    E.op("vector", lambda v, j=j: v.tensor_scalar(
                        xn2[:], utile[:, j * D:(j + 1) * D],
                        stats[:, 16:17], stats[:, 40:41],
                        ALU.subtract, ALU.mult))
                    E.wait_queue("vector", "gdma")
                    E.op("vector", lambda v: v.tensor_copy(stats[:, 44:45], stats[:, 16:17]))
                    E.op("vector", lambda v: v.tensor_copy(stats[:, 45:46], stats[:, 40:41]))
                    nv2 = E.op("vector", lambda v: v.drain(), inc="v")
                    E.wait("gpsimd", "v", nv2)
                    E.dma("gpsimd", ln2_scr[(hoff + i) * 128:(hoff + i + 1) * 128, :],
                          stats[:, 44:46])
                    E.wait("scalar", "v", nv2)
                    E.wait("scalar", "v", E.sems["v"].n)
                    na2 = E.op("scalar", lambda a: a.activation(
                        scr[:], xn2[:], AF.Square, accum_out=stats[:, 46:47]), inc="a")
                    E.op("scalar", lambda a: a.drain())
                    E.wait("vector", "a", na2)
                    E.op("vector", lambda v: v.tensor_scalar(
                        stats[:, 46:47], stats[:, 46:47], 1.0 / D, RMS_EPS,
                        ALU.mult, ALU.add))
                    E.op("vector", lambda v: v.drain())
                    E.op("vector", lambda v: v.reciprocal(stats[:, 47:48], stats[:, 46:47]))
                    nv3 = E.op("vector", lambda v: v.drain(), inc="v")
                    E.wait("scalar", "v", nv3)
                    na3 = E.op("scalar", lambda a: a.activation(
                        stats[:, 47:48], stats[:, 47:48], AF.Sqrt), inc="a")
                    E.op("scalar", lambda a: a.drain())
                    E.wait("vector", "a", na3)
                    E.op("vector", lambda v: v.tensor_scalar(
                        scr[:], xn2[:], stats[:, 47:48], None, ALU.mult))
                    E.op("vector", lambda v: v.drain())
                    nv4 = E.op("vector", lambda v: v.tensor_mul(
                        htile[:], scr[:], ctx["normw_rep"][:]), inc="v")
                    emit_transpose_row(ctx, src=htile, dst=hstg, dst_cols=[
                        k * 128 for k in range(NK)], nv=nv4, ps_t=ps_t)
                    na5 = E.op("scalar", lambda a: a.drain(), inc="a")
                    E.wait("scalar", "a", na5)
                    for k in range(NK):
                        E.dma("scalar",
                              hT_sh[k * 128:(k + 1) * 128,
                                    (hoff + i) * 128:(hoff + i + 1) * 128],
                              hstg[:, k * 128:(k + 1) * 128], sem="adma")
                    E.wait_queue("scalar", "adma")


def emit_transpose_row(ctx, src, dst, dst_cols, nv, ps_t):
    E = ctx["E"]
    E.wait("tensor", "v", nv)
    for k in range(NK):
        slot = 0
        E.wait("tensor", "a", E.sems["a"].n)
        nt = E.op("tensor", lambda t, k=k, slot=slot: t.transpose(
            ps_t[:, slot * 128:slot * 128 + 128],
            src[:, k * 128:(k + 1) * 128], ctx["ident_bf"][:]), inc="t")
        E.wait("scalar", "t", nt)
        na = E.op("scalar", lambda a, k=k, slot=slot: a.copy(
            dst[:, dst_cols[k]:dst_cols[k] + 128],
            ps_t[:, slot * 128:slot * 128 + 128]), inc="a")
    E.op("scalar", lambda a: a.drain())


def emit_ln_stats(ctx, stats, nv, na, dim, eps, only=None, war=True):
    E = ctx["E"]
    if only is None:
        sl_s, sl_q = slice(0, 8), slice(8, 16)
        sl_m, sl_v, sl_t, sl_r = slice(16, 24), slice(24, 32), slice(32, 40), slice(40, 48)
    else:
        i = only
        sl_s, sl_q = slice(i, i + 1), slice(8 + i, 9 + i)
        sl_m, sl_v, sl_t, sl_r = (slice(16 + i, 17 + i), slice(24 + i, 25 + i),
                                  slice(32 + i, 33 + i), slice(40 + i, 41 + i))
    E.wait("vector", "v", nv)
    E.wait("vector", "a", max(na, E.sems["a"].n) if war else na)
    E.op("vector", lambda v: v.tensor_scalar_mul(stats[:, sl_m], stats[:, sl_s], 1.0 / dim))
    E.op("vector", lambda v: v.tensor_scalar_mul(stats[:, sl_v], stats[:, sl_q], 1.0 / dim))
    E.op("vector", lambda v: v.drain())
    E.op("vector", lambda v: v.tensor_mul(stats[:, sl_t], stats[:, sl_m], stats[:, sl_m]))
    E.op("vector", lambda v: v.drain())
    E.op("vector", lambda v: v.tensor_sub(stats[:, sl_v], stats[:, sl_v], stats[:, sl_t]))
    E.op("vector", lambda v: v.drain())
    E.op("vector", lambda v: v.tensor_scalar_add(stats[:, sl_v], stats[:, sl_v], eps))
    E.op("vector", lambda v: v.drain())
    E.op("vector", lambda v: v.reciprocal(stats[:, sl_t], stats[:, sl_v]))
    nvx = E.op("vector", lambda v: v.drain(), inc="v")
    E.wait("scalar", "v", nvx)
    n = E.op("scalar", lambda a: a.activation(stats[:, sl_r], stats[:, sl_t], AF.Sqrt),
             inc="a")
    E.op("scalar", lambda a: a.drain())
    return n



def emit_p2(ctx):
    nc, E = ctx["nc"], ctx["E"]
    hT_full, oA_in, oA_out = ctx["hT_full"], ctx["oA_in"], ctx["oA_out"]

    with contextlib.ExitStack() as _st:
        T = lambda *a: _st.enter_context(nc.sbuf_tensor(*a))
        P = lambda *a: _st.enter_context(nc.psum_tensor(*a))
        UD = U + 3
        whT = T("p2_whT", [128, NK * 1792], BF16)
        hTb = T("p2_hTb", [128, NK * U], BF16)
        rows_all = T("p2_rows", [1, 8 * 1024], F32)
        ps_proj = P("p2_ps_proj", [128, 1024], F32)
        qsq_shared = T("p2_qsq", [128, U], BF16)
        S_ = []
        for s in range(2):
            slv = []
            for pk in range(2):
                dm = T(f"p2_dm{s}{pk}", [128, 512], BF16)
                sv = T(f"p2_sv{s}{pk}", [128, 10 * 128], BF16)
                slv.append(dict(
                    Dst=dm[:, 0:128], Dts=dm[:, 128:256],
                    At=dm[:, 256:384], Ast=dm[:, 384:512],
                    mtmp=T(f"p2_mt{s}{pk}", [128, 256], F32),
                    lam=sv[:, 9 * 128:10 * 128],
                    a=[sv[:, i * 128:(i + 1) * 128] for i in range(5)],
                    x=[sv[:, (5 + i) * 128:(6 + i) * 128] for i in range(2)],
                    m=[sv[:, (7 + i) * 128:(8 + i) * 128] for i in range(2)],
                ))
            qkvp = T(f"p2_qkvp{s}", [128, 8 * UD], BF16)
            d = dict(
                qkvp=qkvp,
                qkvc=T(f"p2_qkvc{s}", [128, 8 * U], BF16),
                qsq=qsq_shared,
                gate_sb=T(f"p2_gate{s}", [128, 4 * U], BF16),
                rows_sb=rows_all[0:1, s * 4 * 1024:(s + 1) * 4 * 1024],
                rows_m=T(f"p2_rowm{s}", [1, 48], F32),
                cerep=T(f"p2_cerep{s}", [128, 24], F32),
                colsBB=T(f"p2_cbb{s}", [128, 16], F32),
                cols_all=T(f"p2_call{s}", [128, 64], F32),
                tails=T(f"p2_tail{s}", [128, 24], BF16),
                pst_all=qkvp[:, 0:1024],
                msb_all=qkvp[:, 1024:2048],
                kbar_all=qkvp[:, 2048:4096],
                vbeta_all=qkvp[:, 4096:8192],
                slv=slv,
                wsb=T(f"p2_w{s}", [128, 512], BF16),
                rsb=T(f"p2_r{s}", [128, 512], BF16),
                osb=T(f"p2_osb{s}", [128, 512], BF16),
                ocp=T(f"p2_ocp{s}", [128, 512], BF16),
                oscr=qsq_shared[:, s * 512:(s + 1) * 512],
                sf0=T(f"p2_sf0{s}", [128, 512], F32),
                sf1=T(f"p2_sf1{s}", [128, 512], F32),
                sb0=T(f"p2_sb0{s}", [128, 512], BF16),
                sb1=T(f"p2_sb1{s}", [128, 512], BF16),
                psB0=ps_proj[:, s * 512:(s + 1) * 512],
                psB1=P(f"p2_psB1_{s}", [128, 512], F32),
                psO=P(f"p2_psO_{s}", [128, 512], F32),
                psX=P(f"p2_psX_{s}", [128, 1024], BF16),
                g={},
            )
            S_.append(d)
        ctx["_p2g"] = {}
        shared = dict(whT=whT, hTb=hTb, ps_proj=ps_proj)

        E.barrier()
        E.wait_tok("sync", ctx["tok_wh"])
        tok_whT = [E.dma("sync", whT[:, k * 1792:(k + 1) * 1792],
                         ctx["wh_bf"][:, k * 128:(k + 1) * 128], transpose=True)
                   for k in range(NK)]
        for t in tok_whT:
            E.wait_tok("tensor", t)

        nz = None
        for s in range(2):
            d = S_[s]
            for buf in (d["sf0"], d["sf1"], d["sb0"], d["sb1"]):
                nz = E.op("gpsimd", lambda g, buf=buf: g.memset(buf[:], 0.0), inc="g")
            nz = E.op("gpsimd", lambda g, d=d: g.memset(
                d["tails"][:], 0.0), inc="g")
        E.wait("vector", "g", nz)
        E.wait("tensor", "g", nz)
        E.wait("scalar", "g", nz)
        E.wait("gpsimd", "cc", ctx["n_ag1"])

        for u in range(NU):
            if u >= P2U:
                ctx.setdefault("n_a2a", {})[u] = E.sems["cc"].n
                continue
            emit_p2_phaseA(ctx, 0, u, S_[0], shared)
            emit_p2_phaseA(ctx, 1, u, S_[1], shared)
            NJ = min(8, P2J)
            queues = {s: [emit_p2_b1(ctx, s, u, j, S_[s], shared)
                          for j in range(NJ)] for s in range(2)}
            active = []
            for s in range(2):
                for _ in range(min(B1W, len(queues[s]))):
                    active.append((s, queues[s].pop(0)))
            while active:
                nxt = []
                for s, gen in active:
                    if next(gen, True) is True:
                        if queues[s]:
                            nxt.append((s, queues[s].pop(0)))
                    else:
                        nxt.append((s, gen))
                active = nxt
            g0 = emit_p2_b2(ctx, 0, u, S_[0], shared)
            g1 = emit_p2_b2(ctx, 1, u, S_[1], shared)
            done0 = done1 = False
            while not (done0 and done1):
                if not done0:
                    done0 = next(g0, True) is True
                if not done1:
                    done1 = next(g1, True) is True
            if P2CC:
                E.wait_queue("gpsimd", "gdma")
                def g_a2a(g, u=u):
                    with allow_ici_all_to_all():
                        return g.collective_compute(
                            "AllToAll", ALU.bypass,
                            replica_groups=[list(range(NCORE))],
                            ins=[oA_in[u]], outs=[oA_out[u]])
                ctx.setdefault("n_a2a", {})[u] = E.op("gpsimd", g_a2a, inc="cc")
            else:
                ctx.setdefault("n_a2a", {})[u] = E.sems["cc"].n


def emit_p2_phaseA(ctx, s, u, L, SH):
    nc, E = ctx["nc"], ctx["E"]
    whT, hTb, ps_proj = SH["whT"], SH["hTb"], SH["ps_proj"]
    qkvp, qkvc, qsq = L["qkvp"], L["qkvc"], L["qsq"]
    gate_sb, rows_m, cerep = L["gate_sb"], L["rows_m"], L["cerep"]
    psB0 = L["psB0"]
    hT_full = ctx["hT_full"]
    G = ctx["_p2g"]
    g = L["g"]
    UD = U + 3
    b = s

    if G.get("hTb_t") is not None:
        E.wait("sync", "t", G["hTb_t"])
        E.wait("scalar", "t", G["hTb_t"])
    tok_h = []
    for k in range(NK):
        toks = []
        for jj in range(NCORE):
            eng, q = (("sync", None) if jj % 2 == 0 else ("scalar", "adma"))
            toks.append(E.dma(
                eng,
                hTb[:, k * U + jj * 128:k * U + (jj + 1) * 128],
                hT_full[jj * 1024 + k * 128: jj * 1024 + (k + 1) * 128,
                        b * 1024 + u * 128: b * 1024 + (u + 1) * 128],
                sem=q))
        tok_h.append(toks)

    if g.get("pm_t") is not None:
        E.wait("vector", "t", g["pm_t"])
    E.wait("vector", "a", E.sems["a"].n)
    nh = None
    for p in range(8):
        nh = E.op("vector", lambda v, p=p: v.tensor_copy(
            qkvp[:, p * UD: p * UD + 3],
            L["tails"][:, p * 3:p * 3 + 3]), inc="v")
    E.wait("scalar", "v", nh)
    g["hist_v"] = nh

    if g.get("qkvp_v") is not None:
        E.wait("scalar", "v", g["qkvp_v"])
    pslot = G.setdefault("pslot", {})
    if g.get("pm_t") is not None:
        E.wait("scalar", "t", g["pm_t"])
    for mi, m in enumerate(list(range(8)) + [12, 13]):
        for hn in range(2):
            sl = (mi * 2 + hn) % 2
            if pslot.get(sl) is not None:
                E.wait("tensor", *pslot[sl])
            for k in range(NK):
                if mi == 0 and hn == 0:
                    for tk in tok_h[k]:
                        E.wait_tok("tensor", tk)
                nt = E.op("tensor", lambda t, m=m, k=k, hn=hn, sl=sl: t.matmul(
                    ps_proj[:, sl * 512:sl * 512 + 512],
                    whT[:, k * 1792 + m * 128:k * 1792 + (m + 1) * 128],
                    hTb[:, k * U + hn * 512:k * U + hn * 512 + 512],
                    start=(k == 0), stop=(k == NK - 1)),
                    inc=("t" if k == NK - 1 else None))
            if m < 8:
                E.wait("scalar", "t", nt)
                na = E.op("scalar", lambda a, m=m, hn=hn, sl=sl: a.copy(
                    qkvp[:, m * UD + 3 + hn * 512: m * UD + 3 + hn * 512 + 512],
                    ps_proj[:, sl * 512:sl * 512 + 512]), inc="a")
                pslot[sl] = ("a", na)
            else:
                blk = 0 if m == 12 else 3 * 1024
                E.wait("vector", "t", nt)
                if g.get("rows_t") is not None:
                    E.wait("vector", "t", g["rows_t"])
                nv = E.op("vector", lambda v, L=L, blk=blk, hn=hn, sl=sl: v.tensor_copy(
                    L["rows_sb"][:, blk + hn * 512:blk + hn * 512 + 512],
                    ps_proj[0:1, sl * 512:sl * 512 + 512]), inc="v")
                pslot[sl] = ("v", nv)

    if g.get("gate_v") is not None:
        E.wait("scalar", "v", g["gate_v"])
    for j in range(8):
        sl = j % 2
        if pslot.get(sl) is not None:
            E.wait("tensor", *pslot[sl])
        for k in range(NK):
            nt = E.op("tensor", lambda t, j=j, k=k, sl=sl: t.matmul(
                ps_proj[:, sl * 512:sl * 512 + 512],
                hTb[:, k * U + j * 128:k * U + (j + 1) * 128],
                whT[:, k * 1792 + 1024:k * 1792 + 1536],
                start=(k == 0), stop=(k == NK - 1)),
                inc=("t" if k == NK - 1 else None))
        E.wait("scalar", "t", nt)
        na = E.op("scalar", lambda a, j=j, sl=sl: a.activation(
            gate_sb[:, j * 512:(j + 1) * 512], ps_proj[:, sl * 512:sl * 512 + 512],
            AF.Silu), inc="a")
        pslot[sl] = ("a", na)
    G["hTb_t"] = E.sems["t"].n

    na_all = E.sems["a"].n
    if g.get("qkvc_t") is not None:
        E.wait("vector", "t", g["qkvc_t"])
    E.wait("vector", "a", na_all)
    for tap in range(4):
        for p in range(8):
            base, ob = p * UD, p * U
            if tap == 0:
                E.op("vector", lambda v, p=p, base=base, ob=ob: v.tensor_scalar(
                    qkvc[:, ob:ob + U], qkvp[:, base:base + U],
                    ctx["convw"][:, p * 4:p * 4 + 1], None, ALU.mult))
            else:
                E.op("vector", lambda v, p=p, base=base, ob=ob, tap=tap:
                     v.scalar_tensor_tensor(
                         qkvc[:, ob:ob + U], qkvp[:, base + tap:base + tap + U],
                         ctx["convw"][:, p * 4 + tap:p * 4 + tap + 1],
                         qkvc[:, ob:ob + U], ALU.mult, ALU.add))
        E.op("vector", lambda v: v.drain())
    nconv = E.op("vector", lambda v: v.drain(), inc="v")
    for p in range(8):
        nconv = E.op("vector", lambda v, p=p: v.tensor_copy(
            L["tails"][:, p * 3:p * 3 + 3],
            qkvp[:, p * UD + U: p * UD + U + 3]), inc="v")
    g["qkvp_v"] = nconv
    for p in range(8):
        E.wait("scalar", "v", nconv)
        E.op("scalar", lambda a, p=p: a.activation(
            qkvc[:, p * U:(p + 1) * U], qkvc[:, p * U:(p + 1) * U], AF.Silu))
        E.op("scalar", lambda a: a.drain())
    na_silu = E.op("scalar", lambda a: a.drain(), inc="a")
    g["qkvc_a"] = na_silu

    BL = 1024
    r0 = L["rows_sb"]
    if g.get("rows_t") is not None:
        E.wait("scalar", "t", g["rows_t"])
        E.wait("vector", "t", g["rows_t"])
    E.wait("scalar", "v", E.sems["v"].n)
    E.op("scalar", lambda a: a.activation(r0[:, BL:2 * BL], r0[:, 0:BL],
                                          AF.Identity, bias=ctx["adt_sb"][0:1, 1:2]))
    E.op("scalar", lambda a: a.drain())
    E.op("scalar", lambda a: a.activation(r0[:, 2 * BL:3 * BL], r0[:, BL:2 * BL], AF.Abs))
    E.op("scalar", lambda a: a.drain())
    E.op("scalar", lambda a: a.activation(r0[:, 2 * BL:3 * BL], r0[:, 2 * BL:3 * BL],
                                          AF.Exp, scale=-1.0))
    E.op("scalar", lambda a: a.drain())
    na_r = E.op("scalar", lambda a: a.activation(
        r0[:, 2 * BL:3 * BL], r0[:, 2 * BL:3 * BL], AF.Ln,
        bias=ctx["onec_f32"][0:1, 0:1]), inc="a")
    E.op("scalar", lambda a: a.drain())
    na_b = E.op("scalar", lambda a: a.activation(
        r0[:, 3 * BL:4 * BL], r0[:, 3 * BL:4 * BL], AF.Tanh, scale=0.5), inc="a")
    E.op("scalar", lambda a: a.drain())
    E.wait("vector", "a", na_b)
    E.op("vector", lambda v: v.tensor_scalar(r0[:, 3 * BL:4 * BL], r0[:, 3 * BL:4 * BL],
                                             0.5, 0.5, ALU.mult, ALU.add))
    E.op("vector", lambda v: v.drain())
    E.op("vector", lambda v: v.tensor_scalar_max(r0[:, BL:2 * BL],
                                                 r0[:, BL:2 * BL], 0.0))
    E.op("vector", lambda v: v.drain())
    E.op("vector", lambda v: v.tensor_add(r0[:, BL:2 * BL],
                                          r0[:, BL:2 * BL], r0[:, 2 * BL:3 * BL]))
    E.op("vector", lambda v: v.drain())
    E.op("vector", lambda v: v.tensor_scalar(
        r0[:, 0:BL], r0[:, BL:2 * BL], ctx["negexpA"][0:1, 0:1], None, ALU.mult))
    E.op("vector", lambda v: v.drain())
    E.op("vector", lambda v: v.tensor_tensor_scan(
        r0[:, 2 * BL:3 * BL], r0[:, 0:BL], r0[:, 0:BL], 0.0, ALU.add, ALU.bypass))
    E.op("vector", lambda v: v.drain())
    E.op("vector", lambda v: v.tensor_scalar_mul(r0[:, BL:2 * BL],
                                                 r0[:, 2 * BL:3 * BL], -1.0))
    g["rows_v"] = E.op("vector", lambda v: v.drain(), inc="v")
    E.op("vector", lambda v: v.memset(rows_m[0:1, 0:1], 0.0))
    E.op("vector", lambda v: v.tensor_copy(
        rows_m[0:1, 1:9], r0[:, 2 * BL + 127:3 * BL:128]))
    E.op("vector", lambda v: v.drain())
    E.op("vector", lambda v: v.tensor_sub(rows_m[0:1, 16:24], rows_m[0:1, 1:9],
                                          rows_m[0:1, 0:8]))
    E.op("vector", lambda v: v.drain())
    E.op("vector", lambda v: v.tensor_scalar_mul(
        rows_m[0:1, 32:40], rows_m[0:1, 0:8], -1.0))
    E.op("vector", lambda v: v.drain())
    nv_nbge = E.op("vector", lambda v: v.tensor_copy(
        rows_m[0:1, 40:48], rows_m[0:1, 1:9]), inc="v")
    E.op("vector", lambda v: v.drain())
    E.wait("scalar", "v", nv_nbge)
    na_ce = E.op("scalar", lambda a: a.activation(rows_m[0:1, 24:32],
                                                  rows_m[0:1, 16:24], AF.Exp), inc="a")
    E.op("scalar", lambda a: a.drain())

    pslot = G["pslot"]
    for sl in (0, 1):
        if pslot.get(sl) is not None:
            E.wait("tensor", *pslot[sl])
    E.wait("tensor", "a", na_ce)
    for j in range(8):
        E.op("tensor", lambda t, j=j: t.transpose(
            psB0[:, 256 + 2 * j:257 + 2 * j],
            r0[:, 2 * BL + j * 128:2 * BL + (j + 1) * 128],
            ctx["ident_f32"][0:1, 0:1]))
        E.op("tensor", lambda t, j=j: t.transpose(
            psB0[:, 257 + 2 * j:258 + 2 * j],
            r0[:, 3 * BL + j * 128:3 * BL + (j + 1) * 128],
            ctx["ident_f32"][0:1, 0:1]))
    nt_bb = E.op("tensor", lambda t: t.matmul(
        psB0[:, 272:296], ctx["onesr_f32"][:], rows_m[0:1, 24:48],
        start=True, stop=True), inc="t")
    E.wait("vector", "t", nt_bb)
    if g.get("cer_v") is not None:
        E.wait("vector", "v", g["cer_v"])
    if g.get("cer_a") is not None:
        E.wait("vector", "a", g["cer_a"])
    E.op("vector", lambda v: v.tensor_copy(L["colsBB"][:], psB0[:, 256:272]))
    nv_bb = E.op("vector", lambda v: v.tensor_copy(cerep[:], psB0[:, 272:296]),
                 inc="v")
    E.op("vector", lambda v: v.drain())
    ca, bb = L["cols_all"], L["colsBB"]
    for j in range(8):
        E.op("vector", lambda v, j=j: v.tensor_scalar_mul(
            ca[:, 8 * j + 3:8 * j + 4], bb[:, 2 * j:2 * j + 1], -1.0))
    nv_nb = E.op("vector", lambda v: v.drain(), inc="v")
    E.wait("scalar", "v", max(nv_bb, nv_nb))
    for j in range(8):
        E.op("scalar", lambda a, j=j: a.activation(
            ca[:, 8 * j:8 * j + 1], bb[:, 2 * j:2 * j + 1], AF.Exp,
            bias=cerep[:, 8 + j:9 + j]))
        E.op("scalar", lambda a, j=j: a.activation(
            ca[:, 8 * j + 2:8 * j + 3], ca[:, 8 * j + 3:8 * j + 4], AF.Exp,
            bias=cerep[:, 16 + j:17 + j]))
    na_cols = E.op("scalar", lambda a: a.drain(), inc="a")
    E.wait("vector", "a", na_cols)
    for j in range(8):
        E.op("vector", lambda v, j=j: v.scalar_tensor_tensor(
            ca[:, 8 * j + 1:8 * j + 2], ca[:, 8 * j:8 * j + 1], -1.0,
            bb[:, 2 * j + 1:2 * j + 2], ALU.mult, ALU.mult))
    nv_cols = E.op("vector", lambda v: v.drain(), inc="v")
    g["cols_v"] = nv_cols
    g["cols_a"] = na_cols
    pslot[s] = ("v", nv_bb)

    pslot = G["pslot"]
    for which, pts in (("q", (0, 1)), ("k", (2, 3))):
        for sl in (0, 1):
            if pslot.get(sl) is not None:
                E.wait("tensor", *pslot[sl])
        nt_last = None
        for i, p in enumerate(pts):
            E.wait("scalar", "a", E.sems["a"].n)
            if i == 1:
                E.wait("scalar", "t", nt_last)
            na_sq = E.op("scalar", lambda a, p=p: a.activation(
                qsq[:], qkvc[:, p * U:(p + 1) * U], AF.Square), inc="a")
            E.op("scalar", lambda a: a.drain())
            E.wait("tensor", "a", na_sq)
            for hn in range(2):
                nt_last = E.op("tensor", lambda t, i=i, hn=hn: t.matmul(
                    ps_proj[0:1, hn * 512:hn * 512 + 512], ctx["onesc_bf"][:],
                    qsq[:, hn * 512:hn * 512 + 512],
                    start=(i == 0), stop=(i == 1)), inc="t")
        rrow = r0[:, 0:BL]
        E.wait("vector", "t", nt_last)
        E.op("vector", lambda v: v.tensor_scalar(
            rrow, ps_proj[0:1, 0:1024], QKN_EPS, None, ALU.add))
        nv_r2 = E.op("vector", lambda v: v.drain(), inc="v")
        E.wait("scalar", "v", nv_r2)
        E.op("scalar", lambda a: a.activation(rrow, rrow, AF.Ln))
        E.op("scalar", lambda a: a.drain())
        na_e = E.op("scalar", lambda a: a.activation(rrow, rrow, AF.Exp, scale=-0.5),
                    inc="a")
        E.op("scalar", lambda a: a.drain())
        E.wait("vector", "a", na_e)
        if which == "q":
            nv_q = E.op("vector", lambda v: v.tensor_scalar_mul(rrow, rrow, 0.0625),
                        inc="v")
            E.op("vector", lambda v: v.drain())
        else:
            nv_q = E.op("vector", lambda v: v.drain(), inc="v")
        E.wait("tensor", "v", nv_q)
        nt_b = None
        for hn in range(2):
            nt_b = E.op("tensor", lambda t, hn=hn: t.matmul(
                ps_proj[:, hn * 512:hn * 512 + 512], ctx["onesr_f32"][:],
                r0[:, hn * 512:hn * 512 + 512],
                start=True, stop=True), inc="t")
        E.wait("vector", "t", nt_b)
        nv_ap = None
        for p in pts:
            nv_ap = E.op("vector", lambda v, p=p: v.tensor_mul(
                qkvc[:, p * U:(p + 1) * U], qkvc[:, p * U:(p + 1) * U],
                ps_proj[:, 0:1024]), inc="v")
            E.op("vector", lambda v: v.drain())
        pslot[0] = ("v", nv_ap)
        pslot[1] = ("v", nv_ap)
        g["qkvc_v"] = nv_ap
def emit_p2_b1(ctx, s, u, j, L, SH):
    nc, E = ctx["nc"], ctx["E"]
    g = L["g"]
    bk = j % 2
    bank = L["psB1"] if bk == 0 else L["psO"]
    ka, kv = ("bp_a", "bp_v") if bk == 0 else ("bq_a", "bq_v")
    sl = L["slv"][bk]
    Dst, Dts, At, Ast = sl["Dst"], sl["Dts"], sl["At"], sl["Ast"]
    mtmp, LamRep = sl["mtmp"], sl["lam"]
    a_slots, x_slots, m_slots = sl["a"], sl["x"], sl["m"]
    psX = L["psX"]
    qkvc = L["qkvc"]
    r0 = L["rows_sb"]
    BL = 1024
    Kf = [qkvc[:, 2 * U + j * 128:2 * U + (j + 1) * 128],
          qkvc[:, 3 * U + j * 128:3 * U + (j + 1) * 128]]
    Qf = [qkvc[:, 0 * U + j * 128:0 * U + (j + 1) * 128],
          qkvc[:, 1 * U + j * 128:1 * U + (j + 1) * 128]]
    Vf = [qkvc[:, (4 + i) * U + j * 128:(4 + i) * U + (j + 1) * 128]
          for i in range(4)]
    cerep, bb, ca = L["cerep"], L["colsBB"], L["cols_all"]
    bgcol = bb[:, 2 * j:2 * j + 1]
    betacol = bb[:, 2 * j + 1:2 * j + 2]
    nbgcol = ca[:, 8 * j + 3:8 * j + 4]
    cekcol = ca[:, 8 * j + 2:8 * j + 3]
    PstO = L["pst_all"][:, j * 128:(j + 1) * 128]
    MsbO = L["msb_all"][:, j * 128:(j + 1) * 128]
    kbarO = L["kbar_all"][:, j * 256:(j + 1) * 256]
    vbetaO = L["vbeta_all"][:, j * 512:(j + 1) * 512]

    if g.get(ka) is not None:
        E.wait("tensor", "a", g[ka])
    if g.get(kv) is not None:
        E.wait("tensor", "v", g[kv])
    E.wait("tensor", "v", max(g["rows_v"], g["cols_v"], g["qkvc_v"]))
    E.wait("tensor", "a", g["qkvc_a"])
    E.op("tensor", lambda t: t.matmul(
        bank[:, 0:128], ctx["onesr_f32"][:],
        r0[:, 2 * BL + j * 128:2 * BL + (j + 1) * 128], start=True, stop=True))
    E.op("tensor", lambda t: t.matmul(
        bank[:, 128:256], ctx["onesr_f32"][:],
        r0[:, BL + j * 128:BL + (j + 1) * 128], start=True, stop=True))
    E.op("tensor", lambda t: t.matmul(bank[:, 256:384], Kf[0], Kf[0],
                                      start=True, stop=False))
    E.op("tensor", lambda t: t.matmul(bank[:, 256:384], Kf[1], Kf[1],
                                      start=False, stop=True))
    E.op("tensor", lambda t: t.matmul(bank[:, 384:512], Kf[0], Qf[0],
                                      start=True, stop=False))
    nt_kq = E.op("tensor", lambda t: t.matmul(bank[:, 384:512], Kf[1], Qf[1],
                                              start=False, stop=True), inc="t")
    g["rows_t"] = nt_kq
    yield

    E.wait("vector", "t", nt_kq)
    if g.get(f"mt{bk}_a") is not None:
        E.wait("vector", "a", g[f"mt{bk}_a"])
    E.op("vector", lambda v: v.tensor_add(mtmp[:, 0:128], bank[:, 0:128],
                                          ctx["mask_st"][:]))
    nv_mt = E.op("vector", lambda v: v.tensor_add(
        mtmp[:, 128:256], bank[:, 128:256], ctx["mask_ts"][:]), inc="v")
    E.op("vector", lambda v: v.drain())
    yield

    E.wait("scalar", "v", max(nv_mt, g["cols_v"]))
    if g.get(f"dm{bk}_v") is not None:
        E.wait("scalar", "v", g[f"dm{bk}_v"])
    if g.get(f"lam{bk}_v") is not None:
        E.wait("scalar", "v", g[f"lam{bk}_v"])
    E.op("scalar", lambda a: a.activation(Dst, mtmp[:, 0:128], AF.Exp,
                                          bias=nbgcol))
    E.op("scalar", lambda a: a.activation(Dts, mtmp[:, 128:256], AF.Exp,
                                          bias=bgcol))
    na_lam = E.op("scalar", lambda a: a.activation(
        LamRep, bank[:, 0:128], AF.Exp, bias=cerep[:, 8 + j:9 + j]), inc="a")
    E.op("scalar", lambda a: a.drain())
    g[f"mt{bk}_a"] = na_lam
    g["cer_a"] = max(g.get("cer_a") or 0, na_lam)
    yield

    E.wait("vector", "a", na_lam)
    if g.get("pm_t") is not None:
        E.wait("vector", "t", g["pm_t"])
    if g.get(f"sl{bk}_t") is not None:
        E.wait("vector", "t", g[f"sl{bk}_t"])
    E.op("vector", lambda v: v.scalar_tensor_tensor(
        At, bank[:, 256:384], betacol, Dts, ALU.mult, ALU.mult))
    E.op("vector", lambda v: v.drain())
    nv_at = E.op("vector", lambda v: v.tensor_mul(PstO, bank[:, 384:512], Dst),
                 inc="v")
    E.op("vector", lambda v: v.drain())
    E.op("vector", lambda v: v.tensor_mul(Qf[0], Qf[0], LamRep))
    nv_qh = E.op("vector", lambda v: v.tensor_mul(Qf[1], Qf[1], LamRep), inc="v")
    E.op("vector", lambda v: v.drain())
    g[f"lam{bk}_v"] = nv_qh
    g[f"dm{bk}_v"] = nv_at
    g["cer_v"] = max(g.get("cer_v") or 0, nv_at)
    yield

    E.wait("tensor", "v", nv_at)
    if g.get("psx_a") is not None:
        E.wait("tensor", "a", g["psx_a"])
    if g.get("psx_v") is not None:
        E.wait("tensor", "v", g["psx_v"])
    E.op("tensor", lambda t: t.transpose(psX[:, 0:128], At, ctx["ident_bf"][:]))
    E.op("tensor", lambda t: t.transpose(psX[:, 128:256], Kf[0],
                                         ctx["ident_bf"][:]))
    E.op("tensor", lambda t: t.transpose(psX[:, 256:384], Kf[1],
                                         ctx["ident_bf"][:]))
    for i in range(3):
        E.op("tensor", lambda t, i=i: t.transpose(
            psX[:, 384 + i * 128:512 + i * 128], Vf[i], ctx["ident_bf"][:]))
    nt_tr = E.op("tensor", lambda t: t.transpose(
        psX[:, 768:896], Vf[3], ctx["ident_bf"][:]), inc="t")
    E.wait("scalar", "t", nt_tr)
    if g.get(f"sl{bk}_t") is not None:
        E.wait("scalar", "t", g[f"sl{bk}_t"])
    na_ast = E.op("scalar", lambda a: a.copy(Ast, psX[:, 0:128]), inc="a")
    E.op("scalar", lambda a: a.drain())
    E.wait("vector", "t", nt_tr)
    E.op("vector", lambda v: v.tensor_copy(kbarO, psX[:, 128:384]))
    E.op("vector", lambda v: v.tensor_copy(vbetaO, psX[:, 384:896]))
    E.op("vector", lambda v: v.drain())
    E.op("vector", lambda v: v.tensor_scalar(
        kbarO, kbarO, cekcol, None, ALU.mult))
    nv_tr = E.op("vector", lambda v: v.tensor_scalar(
        vbetaO, vbetaO, betacol, None, ALU.mult), inc="v")
    E.op("vector", lambda v: v.drain())
    g["psx_a"] = na_ast
    g["psx_v"] = nv_tr
    g["cer_v"] = max(g.get("cer_v") or 0, nv_tr)
    yield

    a_pows = [At]
    cur_x, cur_a = Ast, At
    last_na = None
    E.wait("tensor", "a", na_ast)
    E.wait("tensor", "v", nv_at)
    if g.get(f"m0{bk}_v") is not None:
        E.wait("scalar", "v", g[f"m0{bk}_v"])
    for lvl in range(6):
        if last_na is not None:
            E.wait("tensor", "a", last_na)
        last = lvl == 5
        new_x = x_slots[lvl % 2]
        if not last:
            E.op("tensor", lambda t, cur_x=cur_x, cur_a=cur_a: t.matmul(
                bank[:, 0:128], cur_a, cur_x, start=True, stop=True))
            nt_sq = E.op("tensor", lambda t, cur_x=cur_x, cur_a=cur_a: t.matmul(
                bank[:, 128:256], cur_x, cur_a, start=True, stop=True), inc="t")
            new_a = a_slots[lvl]
            E.wait("scalar", "t", nt_sq)
            E.op("scalar", lambda a, new_x=new_x: a.copy(new_x, bank[:, 0:128]))
            na_sq = E.op("scalar", lambda a, new_a=new_a: a.copy(
                new_a, bank[:, 128:256]), inc="a")
            E.op("scalar", lambda a: a.drain())
            last_na = na_sq
            a_pows.append(new_a)
            cur_x, cur_a = new_x, new_a
        else:
            nt_sq = E.op("tensor", lambda t, cur_x=cur_x, cur_a=cur_a: t.matmul(
                bank[:, 0:128], cur_a, cur_x, start=True, stop=True), inc="t")
            E.wait("scalar", "t", nt_sq)
            na_sq = E.op("scalar", lambda a, new_x=new_x: a.copy(
                new_x, bank[:, 0:128]), inc="a")
            E.op("scalar", lambda a: a.drain())
            last_na = na_sq
            cur_x = new_x
        yield

    E.wait("vector", "a", last_na)
    if g.get(f"sl{bk}_t") is not None:
        E.wait("vector", "t", g[f"sl{bk}_t"])
    nv_m = E.op("vector", lambda v, cur_x=cur_x: v.tensor_add(
        m_slots[0], cur_x, ctx["ident_bf"][:]), inc="v")
    E.op("vector", lambda v: v.drain())
    g[f"m0{bk}_v"] = nv_m
    m_cur = m_slots[0]
    prods = [a_pows[5], a_pows[4], a_pows[3], a_pows[2], a_pows[1], a_pows[0]]
    E.wait("tensor", "a", last_na)
    nt_p = None
    for pi, Ap in enumerate(prods):
        sign = -1.0 if pi == len(prods) - 1 else 1.0
        E.wait("tensor", "v", nv_m)
        nt_p = E.op("tensor", lambda t, Ap=Ap, m_cur=m_cur: t.matmul(
            bank[:, 256:384], Ap, m_cur, start=True, stop=True), inc="t")
        m_next = MsbO if pi == len(prods) - 1 else m_slots[1 - pi % 2]
        E.wait("vector", "t", nt_p)
        nv_m = E.op("vector", lambda v, m_next=m_next, m_cur=m_cur, sign=sign:
                    v.scalar_tensor_tensor(
                        m_next, bank[:, 256:384], sign, m_cur,
                        ALU.mult, ALU.add), inc="v")
        E.op("vector", lambda v: v.drain())
        m_cur = m_next
        yield
    g[f"sl{bk}_t"] = nt_p
    g[ka] = last_na
    g[kv] = nv_m
    return


def emit_p2_b2(ctx, s, u, L, SH):
    nc, E = ctx["nc"], ctx["E"]
    g = L["g"]
    bankP, bankQ = L["psB1"], L["psO"]
    qkvc, gate_sb = L["qkvc"], L["gate_sb"]
    cerep, ca = L["cerep"], L["cols_all"]
    wsb, rsb, osb, ocp, oscr = L["wsb"], L["rsb"], L["osb"], L["ocp"], L["oscr"]
    sf0, sf1, sb0, sb1 = L["sf0"], L["sf1"], L["sb0"], L["sb1"]
    oA_in = ctx["oA_in"]
    b = s
    NJ = min(8, P2J)
    pend = []
    for j in range(NJ):
        Kf = [qkvc[:, 2 * U + j * 128:2 * U + (j + 1) * 128],
              qkvc[:, 3 * U + j * 128:3 * U + (j + 1) * 128]]
        Qf = [qkvc[:, 0 * U + j * 128:0 * U + (j + 1) * 128],
              qkvc[:, 1 * U + j * 128:1 * U + (j + 1) * 128]]
        PstO = L["pst_all"][:, j * 128:(j + 1) * 128]
        MsbO = L["msb_all"][:, j * 128:(j + 1) * 128]
        kbarO = L["kbar_all"][:, j * 256:(j + 1) * 256]
        vbetaO = L["vbeta_all"][:, j * 512:(j + 1) * 512]
        nblcol = ca[:, 8 * j + 1:8 * j + 2]
        mscol = ca[:, 8 * j + 4:8 * j + 5]
        rscol = ca[:, 8 * j + 5:8 * j + 6]

        if g.get("state_a") is not None:
            E.wait("tensor", "a", g["state_a"])
        if g.get("bp_a") is not None:
            E.wait("tensor", "a", g["bp_a"])
        if g.get("bp_v") is not None:
            E.wait("tensor", "v", g["bp_v"])
        E.op("tensor", lambda t, Kf=Kf: t.matmul(bankP[:], Kf[0], sb0[:],
                                                 start=True, stop=False))
        nt_ks = E.op("tensor", lambda t, Kf=Kf: t.matmul(
            bankP[:], Kf[1], sb1[:], start=False, stop=True), inc="t")
        yield

        E.wait("vector", "t", nt_ks)
        if g.get("r_t") is not None:
            E.wait("vector", "t", g["r_t"])
        nv_r = E.op("vector", lambda v, nblcol=nblcol, vbetaO=vbetaO:
                    v.scalar_tensor_tensor(
                        rsb[:], bankP[:], nblcol, vbetaO,
                        ALU.mult, ALU.add), inc="v")
        E.op("vector", lambda v: v.drain())
        yield

        E.wait("tensor", "v", nv_r)
        if g.get("bq_a") is not None:
            E.wait("tensor", "a", g["bq_a"])
        if g.get("bq_v") is not None:
            E.wait("tensor", "v", g["bq_v"])
        nt_w = E.op("tensor", lambda t, MsbO=MsbO: t.matmul(
            bankQ[:], MsbO, rsb[:], start=True, stop=True), inc="t")
        g["r_t"] = nt_w
        E.wait("scalar", "t", nt_w)
        if g.get("w_t") is not None:
            E.wait("scalar", "t", g["w_t"])
        na_w = E.op("scalar", lambda a: a.copy(wsb[:], bankQ[:]), inc="a")
        E.op("scalar", lambda a: a.drain())
        g["bq_a"] = na_w
        if pend:
            pend.pop(0)()
        yield

        E.wait("tensor", "a", na_w)
        E.op("tensor", lambda t, PstO=PstO: t.matmul(
            bankP[:], PstO, wsb[:], start=True, stop=False))
        E.op("tensor", lambda t, Qf=Qf: t.matmul(
            bankP[:], Qf[0], sb0[:], start=False, stop=False))
        nt_o = E.op("tensor", lambda t, Qf=Qf: t.matmul(
            bankP[:], Qf[1], sb1[:], start=False, stop=True), inc="t")
        nt_ds0 = E.op("tensor", lambda t, kbarO=kbarO: t.matmul(
            bankQ[:], kbarO[:, 0:128], wsb[:], start=True, stop=True), inc="t")
        g["qkvc_t"] = nt_o
        yield

        E.wait("vector", "t", nt_ds0)
        nv_s0 = E.op("vector", lambda v, j=j: v.scalar_tensor_tensor(
            sf0[:], sf0[:], cerep[:, j:j + 1], bankQ[:], ALU.mult, ALU.add),
            inc="v")
        E.op("vector", lambda v: v.drain())
        E.wait("vector", "t", nt_o)
        if g.get("ocp_a") is not None:
            E.wait("vector", "a", g["ocp_a"])
        nv_oc = E.op("vector", lambda v: v.tensor_copy(ocp[:], bankP[:]),
                     inc="v")
        E.op("vector", lambda v: v.drain())
        g["bp_v"] = nv_oc
        yield

        E.wait("tensor", "v", nv_s0)
        E.wait("scalar", "t", nt_o)
        na_sq2 = E.op("scalar", lambda a, mscol=mscol: a.activation(
            oscr[:], bankP[:], AF.Square, accum_out=mscol), inc="a")
        E.op("scalar", lambda a: a.drain())
        nt_ds1 = E.op("tensor", lambda t, kbarO=kbarO: t.matmul(
            bankQ[:], kbarO[:, 128:256], wsb[:], start=True, stop=True),
            inc="t")
        g["w_t"] = nt_ds1
        g["ocp_a"] = na_sq2
        g["bp_a"] = max(g.get("bp_a") or 0, na_sq2)
        yield

        E.wait("vector", "t", nt_ds1)
        nv_s1 = E.op("vector", lambda v, j=j: v.scalar_tensor_tensor(
            sf1[:], sf1[:], cerep[:, j:j + 1], bankQ[:], ALU.mult, ALU.add),
            inc="v")
        E.op("vector", lambda v: v.drain())
        g["bq_v"] = nv_s1
        g["cer_v"] = max(g.get("cer_v") or 0, nv_s1)
        E.wait("scalar", "v", nv_s1)
        E.op("scalar", lambda a: a.copy(sb0[:], sf0[:]))
        na_sb = E.op("scalar", lambda a: a.copy(sb1[:], sf1[:]), inc="a")
        E.op("scalar", lambda a: a.drain())
        g["state_a"] = na_sb
        yield

        def epi(j=j, na_sq2=na_sq2, mscol=mscol, rscol=rscol):
            E.wait("vector", "a", na_sq2)
            E.op("vector", lambda v: v.tensor_scalar(
                mscol, mscol, 1.0 / DV, GN_EPS, ALU.mult, ALU.add))
            nv_ms = E.op("vector", lambda v: v.drain(), inc="v")
            E.wait("scalar", "v", nv_ms)
            E.op("scalar", lambda a: a.activation(rscol, mscol, AF.Ln))
            E.op("scalar", lambda a: a.drain())
            na_rs = E.op("scalar", lambda a: a.activation(
                rscol, rscol, AF.Exp, scale=-0.5), inc="a")
            E.op("scalar", lambda a: a.drain())
            g["cer_a"] = max(g.get("cer_a") or 0, na_rs)
            E.wait("vector", "a", na_rs)
            if g.get("osb_tok") is not None:
                E.wait_tok("vector", g["osb_tok"])
            nv_o = E.op("vector", lambda v: v.scalar_tensor_tensor(
                osb[:], ocp[:], rscol, gate_sb[:, j * 512:(j + 1) * 512],
                ALU.mult, ALU.mult), inc="v")
            E.op("vector", lambda v: v.drain())
            g["gate_v"] = nv_o
            E.wait("gpsimd", "v", nv_o)
            row0 = j * 256 + b * 128
            g["osb_tok"] = E.dma("gpsimd", oA_in[u][row0:row0 + 128, :], osb[:])
        pend.append(epi)
        yield
    for epi in pend:
        epi()
    g["pm_t"] = E.sems["t"].n
    return


def emit_p3(ctx):
    nc, E = ctx["nc"], ctx["E"]
    oA_out, out_sh = ctx["oA_out"], ctx["out_sh"]
    u_scr, ln2_scr, x3_scr = ctx["u_scr"], ctx["ln2_scr"], ctx["x3_scr"]

    with contextlib.ExitStack() as _st:
        owT = _st.enter_context(nc.sbuf_tensor("p3_owT", [128, 32 * 1024], BF16))
        oT = _st.enter_context(nc.sbuf_tensor("p3_oT", [128, 2 * 32 * 128], BF16))
        utile2 = _st.enter_context(nc.sbuf_tensor("p3_u", [128, 2 * D], F32))
        l22 = _st.enter_context(nc.sbuf_tensor("p3_l2", [128, 4], F32))
        ztile2 = _st.enter_context(nc.sbuf_tensor("p3_z", [128, 2 * D], F32))
        x3t2 = _st.enter_context(nc.sbuf_tensor("p3_x3", [128, 2 * D], F32))
        ps = _st.enter_context(nc.psum_tensor("p3_ps", [128, 1024], F32))
        E.barrier()
        for t in ctx["tok_owsc"]:
            E.wait_tok("sync", t)
        tow = [E.dma("sync", owT[:, kf * 1024:(kf + 1) * 1024],
                     ctx["ow_sc"][:, kf * 128:(kf + 1) * 128], transpose=True)
               for kf in range(32)]
        for t in tow:
            E.wait_tok("tensor", t)

        def load_oT(tt):
            b, uu = divmod(tt, NU)
            buf = (tt % 2) * 4096
            E.wait("sync", "cc", ctx["n_a2a"][uu])
            E.wait("scalar", "cc", ctx["n_a2a"][uu])
            tks = []
            for i in range(H):
                for fb in range(4):
                    r0 = i * 256 + b * 128
                    tks.append(E.dma(
                        "sync",
                        oT[:, buf + (i * 4 + fb) * 128:buf + (i * 4 + fb + 1) * 128],
                        oA_out[uu][r0:r0 + 128, fb * 128:(fb + 1) * 128],
                        transpose=True))
            return tks

        tks = load_oT(0)
        nt_prev = None
        nv_x3p = [None, None]
        t_x3p = [None, None]
        for tt in range(NTT):
            buf = (tt % 2) * 4096
            if tt + 1 < NTT:
                if nt_prev is not None:
                    E.wait("sync", "t", nt_prev)
                    E.wait("scalar", "t", nt_prev)
                tks_next = load_oT(tt + 1)
            sb = tt % 2
            utile = utile2[:, sb * D:(sb + 1) * D]
            ztile = ztile2[:, sb * D:(sb + 1) * D]
            x3t = x3t2[:, sb * D:(sb + 1) * D]
            l2 = l22[:, sb * 2:(sb + 1) * 2]
            if nv_x3p[sb] is not None:
                E.wait("gpsimd", "v", nv_x3p[sb])
            t_u = E.dma("gpsimd", utile[:], u_scr[tt * 128:(tt + 1) * 128, :])
            t_l2 = E.dma("gpsimd", l2[:], ln2_scr[tt * 128:(tt + 1) * 128, :])
            E.wait("tensor", "v", E.sems["v"].n)
            for kf in range(32):
                if kf < len(tks):
                    E.wait_tok("tensor", tks[kf])
                for hn in range(2):
                    nt = E.op("tensor", lambda t, kf=kf, hn=hn, buf=buf: t.matmul(
                        ps[:, hn * 512:hn * 512 + 512],
                        oT[:, buf + kf * 128:buf + (kf + 1) * 128],
                        owT[:, kf * 1024 + hn * 512:kf * 1024 + hn * 512 + 512],
                        start=(kf == 0), stop=(kf == 31)),
                        inc=("t" if kf == 31 and hn == 1 else None))
            nt_prev = nt
            if tt + 1 < NTT:
                tks = tks_next
            E.wait_tok("vector", t_u)
            E.wait_tok("vector", t_l2)
            E.op("vector", lambda v, ztile=ztile, utile=utile, l2=l2: v.tensor_scalar(
                ztile[:], utile[:], l2[:, 0:1], l2[:, 1:2], ALU.subtract, ALU.mult))
            E.op("vector", lambda v: v.drain())
            E.op("vector", lambda v, ztile=ztile, utile=utile: v.tensor_add(
                ztile[:], ztile[:], utile[:]))
            E.op("vector", lambda v: v.drain())
            if t_x3p[sb] is not None:
                E.wait_tok("vector", t_x3p[sb])
            E.wait("vector", "t", nt)
            nv_x3 = E.op("vector", lambda v, x3t=x3t, ztile=ztile: v.tensor_add(
                x3t[:], ztile[:], ps[:, 0:1024]), inc="v")
            E.op("vector", lambda v: v.drain())
            nv_x3p[sb] = nv_x3
            E.wait("gpsimd", "v", nv_x3)
            t_x3p[sb] = E.dma("gpsimd", x3_scr[tt * 128:(tt + 1) * 128, :], x3t[:])

    with contextlib.ExitStack() as _st:
        xb = _st.enter_context(nc.sbuf_tensor("p3_x", [128, 2 * D], F32))
        xn1 = _st.enter_context(nc.sbuf_tensor("p3_xn1", [128, D], BF16))
        scr = _st.enter_context(nc.sbuf_tensor("p3_scr", [128, D], F32))
        xn1T = _st.enter_context(nc.sbuf_tensor("p3_xn1T", [128, NK * U], BF16))
        wgT = _st.enter_context(nc.sbuf_tensor("p3_wgT", [128, NK * FH], BF16))
        wuT = _st.enter_context(nc.sbuf_tensor("p3_wuT", [128, NK * FH], BF16))
        hffT = _st.enter_context(nc.sbuf_tensor("p3_hffT", [128, NM * U], BF16))
        wdTk = _st.enter_context(nc.sbuf_tensor("p3_wdTk", [128, 2 * D], BF16))
        stats = _st.enter_context(nc.sbuf_tensor("p3_stats", [128, 48], F32))
        sgate = _st.enter_context(nc.sbuf_tensor("p3_sg", [128, U], BF16))
        otile = _st.enter_context(nc.sbuf_tensor("p3_out", [128, 2 * D], F32))
        ps_big = _st.enter_context(nc.psum_tensor("p3_ps_big", [128, 2048], F32))
        ps_up2 = _st.enter_context(nc.psum_tensor("p3_ps_up2", [128, 1024], F32))
        ps_t = _st.enter_context(nc.psum_tensor("p3_ps_t", [128, 512], BF16))
        E.barrier()
        ng0 = E.op("gpsimd", lambda g: g.memset(stats[:], 0.0), inc="g")
        E.wait("vector", "g", ng0)
        E.wait("scalar", "g", ng0)
        wu_toks = []
        for k in range(NK):
            wu_toks.append(E.dma("sync", wgT[:, k * FH:(k + 1) * FH],
                  ctx["wg_bf"][:, k * 128:(k + 1) * 128], transpose=True))
            wu_toks.append(E.dma("sync", wuT[:, k * FH:(k + 1) * FH],
                          ctx["wu_bf"][:, k * 128:(k + 1) * 128], transpose=True))

        tok_xb = [None, None]
        rd_xb = [0, 0]
        rd_xb_a = [0, 0]
        for half in range(2):
            hoff = half * 8
            E.barrier()
            if half == 1:
                wu_toks = []
                for k in range(NK):
                    wu_toks.append(E.dma("sync", wuT[:, k * FH:(k + 1) * FH],
                                   ctx["wu_bf"][:, k * 128:(k + 1) * 128],
                                   transpose=True))
            chain_v = [0, 0]
            stA = {}

            def finishB(i):
                sl = i % 2
                nv0, na0 = stA[i]
                n_rstd = emit_ln_stats(ctx, stats, nv0, na0, D, LN_EPS,
                                       only=sl, war=False)
                chain_v[sl] = E.sems["v"].n
                E.wait("vector", "a", n_rstd)
                nv = E.op("vector", lambda v, sl=sl: v.tensor_scalar(
                    xn1[:], xb[:, sl * D:(sl + 1) * D],
                    stats[:, 16 + sl:17 + sl], stats[:, 40 + sl:41 + sl],
                    ALU.subtract, ALU.mult), inc="v")
                rd_xb[sl] = nv
                E.wait("sync", "v", nv)
                emit_transpose_row(ctx, src=xn1, dst=xn1T, dst_cols=[
                    (k * U + i * 128) for k in range(NK)], nv=nv, ps_t=ps_t)

            for i in range(8):
                sl = i % 2
                if tok_xb[sl] is not None:
                    E.wait_tok("sync", tok_xb[sl])
                if rd_xb[sl]:
                    E.wait("sync", "v", rd_xb[sl])
                if rd_xb_a[sl]:
                    E.wait("sync", "a", rd_xb_a[sl])
                n_ldx = E.dma("sync", xb[:, sl * D:(sl + 1) * D],
                              x3_scr[(hoff + i) * 128:(hoff + i + 1) * 128, :])
                tok_xb[sl] = n_ldx
                E.wait_tok("vector", n_ldx)
                E.op("vector", lambda v, sl=sl: v.reduce_sum(
                    stats[:, sl:sl + 1], xb[:, sl * D:(sl + 1) * D], axis=AX))
                nv0 = E.op("vector", lambda v: v.drain(), inc="v")
                E.wait_tok("scalar", n_ldx)
                E.wait("scalar", "v", chain_v[sl])
                na0 = E.op("scalar", lambda a, sl=sl: a.activation(
                    scr[:], xb[:, sl * D:(sl + 1) * D], AF.Square,
                    accum_out=stats[:, 8 + sl:9 + sl]), inc="a")
                E.op("scalar", lambda a: a.drain())
                rd_xb_a[sl] = na0
                stA[i] = (nv0, na0)
                if i > 0:
                    finishB(i - 1)
            finishB(7)
            for _t in wu_toks:
                E.wait_tok("tensor", _t)
            E.wait("tensor", "a", E.sems["a"].n)
            na_sil = None
            nv_last = None
            nv_up = [None, None]
            for m in range(NM):
                upps = ps_big[:, 1024:2048] if m % 2 == 0 else ps_up2[:, 0:1024]
                if na_sil is not None:
                    E.wait("tensor", "a", na_sil)
                for hn in range(2):
                    for k in range(NK):
                        E.op("tensor", lambda t, m=m, k=k, hn=hn: t.matmul(
                            ps_big[:, hn * 512:hn * 512 + 512],
                            wgT[:, k * FH + m * 128:k * FH + (m + 1) * 128],
                            xn1T[:, k * U + hn * 512:k * U + hn * 512 + 512],
                            start=(k == 0), stop=(k == NK - 1)))
                if nv_up[m % 2] is not None:
                    E.wait("tensor", "v", nv_up[m % 2])
                for hn in range(2):
                    for k in range(NK):
                        ntl = E.op("tensor", lambda t, m=m, k=k, hn=hn, upps=upps: t.matmul(
                            upps[:, hn * 512:hn * 512 + 512],
                            wuT[:, k * FH + m * 128:k * FH + (m + 1) * 128],
                            xn1T[:, k * U + hn * 512:k * U + hn * 512 + 512],
                            start=(k == 0), stop=(k == NK - 1)), inc="t")
                nt = ntl
                E.wait("scalar", "t", nt)
                if nv_last is not None:
                    E.wait("scalar", "v", nv_last)
                E.op("scalar", lambda a: a.activation(sgate[:], ps_big[:, 0:U],
                                                      AF.Silu, scale=QS))
                na_sil = E.op("scalar", lambda a: a.drain(), inc="a")
                E.wait("vector", "a", na_sil)
                nv_up[m % 2] = E.op("vector", lambda v, m=m, upps=upps: v.scalar_tensor_tensor(
                    hffT[:, m * U:(m + 1) * U], upps[:], QS, sgate[:],
                    ALU.mult, ALU.mult), inc="v")
                nv_last = nv_up[m % 2]
            wd_toks = []
            E.wait("sync", "t", E.sems["t"].n)
            E.wait("scalar", "t", E.sems["t"].n)
            for k in range(NM):
                wd_toks.append(E.dma("sync", wuT[:, k * 1024:(k + 1) * 1024],
                                     ctx["wd_bf"][:, k * 128:(k + 1) * 128],
                                     transpose=True))
            E.wait("tensor", "a", E.sems["a"].n)
            for p in range(4):
                i0, i1 = 2 * p, 2 * p + 1
                E.wait("tensor", "v", E.sems["v"].n)
                for k in range(NM):
                    if p == 0:
                        E.wait_tok("tensor", wd_toks[k])
                    for jj, i in ((0, i0), (1, i1)):
                        for hn in range(2):
                            ntl = E.op("tensor", lambda t, k=k, jj=jj, i=i, hn=hn: t.matmul(
                                ps_big[:, jj * 1024 + hn * 512:jj * 1024 + hn * 512 + 512],
                                hffT[:, k * U + i * 128:k * U + (i + 1) * 128],
                                wuT[:, k * 1024 + hn * 512:k * 1024 + hn * 512 + 512],
                                start=(k == 0), stop=(k == NM - 1)), inc="t")
                nt_d = ntl
                for jj, i in ((0, i0), (1, i1)):
                    if tok_xb[jj] is not None:
                        E.wait_tok("sync", tok_xb[jj])
                    if rd_xb[jj]:
                        E.wait("sync", "v", rd_xb[jj])
                    n_rx = E.dma("sync", xb[:, jj * D:(jj + 1) * D],
                                 x3_scr[(hoff + i) * 128:(hoff + i + 1) * 128, :])
                    tok_xb[jj] = n_rx
                    E.wait("vector", "t", nt_d)
                    E.wait_tok("vector", n_rx)
                    nv_u = E.op("vector", lambda v, jj=jj: v.scalar_tensor_tensor(
                        otile[:, jj * D:(jj + 1) * D],
                        ps_big[:, jj * 1024:(jj + 1) * 1024],
                        0.5 * QS, xb[:, jj * D:(jj + 1) * D],
                        ALU.mult, ALU.add), inc="v")
                    rd_xb[jj] = nv_u
                    E.op("vector", lambda v: v.drain())
                    E.wait("gpsimd", "v", nv_u)
                    t_o = E.dma("gpsimd", out_sh[(hoff + i) * 128:(hoff + i + 1) * 128, :],
                                otile[:, jj * D:(jj + 1) * D])
                    E.wait_tok("vector", t_o)



_NC_CACHE = {}


def _get_nc():
    if DEBUG not in _NC_CACHE:
        _NC_CACHE[DEBUG] = build_program()
    return _NC_CACHE[DEBUG]


def shard_inputs(inputs):
    x = np.asarray(inputs["L0_attn_x"])
    q_w = np.asarray(inputs["L0_attn_q_w"])
    k_w = np.asarray(inputs["L0_attn_k_w"])
    v_w = np.asarray(inputs["L0_attn_v_w"])
    g_w = np.asarray(inputs["L0_attn_g_w"])
    a_w = np.asarray(inputs["L0_attn_a_w"])
    b_w = np.asarray(inputs["L0_attn_b_w"])
    A_log = np.asarray(inputs["L0_attn_A_log"])
    dt_b = np.asarray(inputs["L0_attn_dt_bias"])
    q_c = np.asarray(inputs["L0_attn_q_conv"])
    k_c = np.asarray(inputs["L0_attn_k_conv"])
    v_c = np.asarray(inputs["L0_attn_v_conv"])
    norm_w = np.asarray(inputs["L0_attn_norm_w"])
    o_norm = np.asarray(inputs["L0_attn_o_norm_w"])
    o_w = np.asarray(inputs["L0_attn_o_w"])
    wg = np.asarray(inputs["L0_ffn_w_gate"])
    wu = np.asarray(inputs["L0_ffn_w_up"])
    wd = np.asarray(inputs["L0_ffn_w_down"])

    in_maps = []
    for c in range(NCORE):
        xs = np.empty((LT, D), np.float32)
        for tt in range(NTT):
            b, u = divmod(tt, NU)
            xs[tt * 128:(tt + 1) * 128] = x[b, u * U + c * 128: u * U + c * 128 + 128]
        qs = slice(c * DK, (c + 1) * DK)
        vs = slice(c * DV, (c + 1) * DV)
        wh = np.concatenate([
            q_w[qs], k_w[qs], v_w[vs], g_w[vs],
            a_w[c:c + 1], np.zeros((127, D), np.float32),
            b_w[c:c + 1], np.zeros((127, D), np.float32),
        ], 0)
        cw = np.concatenate([q_c[qs], k_c[qs], v_c[vs]], 0)
        cw = cw.reshape(8, 128, 4).transpose(1, 0, 2).reshape(128, 32)
        in_maps.append({
            "x_sh": np.ascontiguousarray(xs),
            "w_gate": wg, "w_up": wu, "w_down": wd,
            "norm_w": norm_w.reshape(1, D).astype(np.float32),
            "w_head": np.ascontiguousarray(wh.astype(np.float32)),
            "conv_w": np.ascontiguousarray(cw.astype(np.float32)),
            "o_w": o_w,
            "o_norm": np.tile(o_norm, H).reshape(1, H * DV).astype(np.float32),
            "adt": np.array([[A_log[c], dt_b[c]]], np.float32),
        })
    return in_maps


def unshard_output(results):
    out = np.empty((B, T, D), np.float32)
    for c in range(NCORE):
        o = results[c]["out_sh"]
        for tt in range(NTT):
            b, u = divmod(tt, NU)
            out[b, u * U + c * 128: u * U + c * 128 + 128] = o[tt * 128:(tt + 1) * 128]
    return out


def kernel(**inputs):
    in_maps = shard_inputs(inputs)
    nc = _get_nc()
    res = run_bass_kernel_spmd(nc, in_maps, core_ids=list(range(NCORE)))
    return unshard_output(res.results)
